# version 03
# baseline (speedup 1.0000x reference)
import contextlib
import math
import numpy as np
import ml_dtypes
import concourse.bass as bass
import concourse.mybir as mybir
from concourse.bass_utils import run_bass_kernel_spmd

F32 = mybir.dt.float32
BF16 = mybir.dt.bfloat16
AF = mybir.ActivationFunctionType
ALU = mybir.AluOpType
AX = mybir.AxisListType

S = 2048
DM = 2048
H, HD, LAT, PD = 16, 128, 512, 64
NE, DI = 16, 1024
NKC = DM // 128
NTC = S // 128
NTS = S // 512
NIC = DI // 128
NLC = LAT // 128
ATT_SCALE = 1.0 / math.sqrt(HD + PD)
BIG = 30000.0
N_CORES = 8
EXP = 17


class Sem:
    __slots__ = ("h", "n")

    def __init__(self, nc, name):
        self.h = nc.alloc_semaphore(name)
        self.n = 0

    def tick(self, ins, k=1):
        ins.then_inc(self.h, k)
        self.n += k
        return self.n


class DSem:

    def __init__(self, nc, name):
        self.s = [Sem(nc, name + "0"), Sem(nc, name + "1")]

    def tick(self, ins, i, k=16):
        return self.s[i % 2].tick(ins, k)

    def wait(self, eng, i, tgt):
        eng.wait_ge(self.s[i % 2].h, tgt)

    def wait_all(self, eng):
        eng.wait_ge(self.s[0].h, self.s[0].n)
        eng.wait_ge(self.s[1].h, self.s[1].n)


class Ctx:
    pass


def wcol(dram2d, c0, w):
    return dram2d[:, c0:c0 + w].rearrange("(kc p) m -> p kc m", p=128)


def build_program(nb):
    nc = bass.Bass("TRN2", target_bir_lowering=False, debug=False)
    c = Ctx()
    c.nc = nc
    c.nb = nb

    c.x = nc.dram_tensor("x", [nb, S, DM], F32, kind="ExternalInput").ap()
    for nm, sh in (("W_DKV", [DM, LAT]), ("W_DQ", [DM, LAT]), ("W_UK", [LAT, DM]),
                   ("W_UV", [LAT, DM]), ("W_UQ", [LAT, DM]), ("W_KR", [DM, PD]),
                   ("W_QR", [LAT, H * PD]), ("W_O", [DM, DM]), ("Wr", [NE, DM]),
                   ("Wg", [NE, DM, DI]), ("Wu", [NE, DM, DI]), ("Wd", [NE, DI, DM]),
                   ("sWg", [DM, DI]), ("sWu", [DM, DI]), ("sWd", [DI, DM]),
                   ("eyef", [128, 128]), ("ropec", [128, S]), ("ropes", [128, S])):
        setattr(c, nm, nc.dram_tensor(nm, sh, F32, kind="ExternalInput").ap())
    c.rbias = nc.dram_tensor("rbias", [NE], F32, kind="ExternalInput").ap()
    c.out = nc.dram_tensor("out", [nb, S, DM], F32, kind="ExternalOutput").ap()

    pes = contextlib.ExitStack()
    c.pes = pes

    def sb(stack, n, sh, d):
        return stack.enter_context(nc.sbuf_tensor(n, sh, d)).ap()

    def psr(stack, n, sh, d):
        return stack.enter_context(nc.psum_tensor(n, sh, d)).ap()

    c.sb, c.psr = sb, psr

    c.xno = sb(pes, "xno", [128, NKC * S], BF16)
    c.acc = sb(pes, "acc", [128, NKC * S], BF16)
    c.eyeb = sb(pes, "eyeb", [128, 128], BF16)
    c.eyes = sb(pes, "eyes", [128, 128], F32)
    c.cosb = sb(pes, "cosb", [128, S], BF16)
    c.sinb = sb(pes, "sinb", [128, S], BF16)
    c.wrt = sb(pes, "wrt", [128, NKC * NE], BF16)
    c.brow = sb(pes, "brow", [1, NE], BF16)
    c.ones_b = sb(pes, "ones_b", [1, 128], BF16)
    c.epsb = sb(pes, "epsb", [128, 1], F32)

    SN = ("sLd", "sStat", "sSq", "sXnT", "sXnEv",
          "sGw", "sUpG", "sUpU", "sSilu", "sH1", "sHg",
          "sDwn", "sAcc", "sGt", "sGtEv", "sRoute", "sLg", "sRtA",
          "sGb", "sGbEv", "sCp", "sCpEv", "sKp", "sRope",
          "sKw", "sKpEv", "sS", "sExp", "sNm", "sPn", "sPT", "sPTEv",
          "sPV", "sOEv", "sWOp", "sWOEv", "sP5T", "sP5Ev", "sWrtEv", "sMst", "sPEvA", "sPEvD")
    for n in SN:
        setattr(c, n, Sem(nc, n))
    c.dX = DSem(nc, "dX")
    c.dWgA = DSem(nc, "dWg")
    c.dWuA = DSem(nc, "dWu")
    c.dWd = DSem(nc, "dWd")
    c.dCw = DSem(nc, "dCw")
    c.dWO = DSem(nc, "dWO")
    c.dOut = DSem(nc, "dOut")

    c.xn_c = lambda kc: c.xno[:, kc * S:(kc + 1) * S]
    c.acc_c = lambda kc: c.acc[:, kc * S:(kc + 1) * S]

    prelude(c)
    for b in range(nb):
        phase_P0(c, b)
        phase_P1(c, b)
        with contextlib.ExitStack() as p23:
            c.cKV = sb(p23, f"cKV{b}", [128, NLC * S], BF16)
            c.cQ = sb(p23, f"cQ{b}", [128, NLC * S], BF16)
            c.kRb = sb(p23, f"kRb{b}", [64, S], BF16)
            phase_P2(c, b)
            phase_P3(c, b)
        phase_P4(c, b)
        phase_P5(c, b)

    with nc.Block(no_gpsimd_drain=True) as B:
        @B.sync
        def _(sy):
            c.dOut.wait_all(sy)

    pes.close()
    return nc


def prelude(c):
    nc = c.nc
    with contextlib.ExitStack() as st:
        ppw = [c.psr(st, f"ppw{i}", [128, 512], F32) for i in range(2)]
        wr_raw = c.sb(st, "wr_raw", [16, DM], F32)
        with nc.Block(no_gpsimd_drain=True) as B:
            @B.gpsimd
            def _(e):
                for dst, src in ((c.eyeb, c.eyef), (c.eyes, c.eyef),
                                 (c.cosb, c.ropec), (c.sinb, c.ropes)):
                    ins = e.dma_start(out=dst[:, :], in_=src[:, :])
                    c.sLd.tick(ins, 16)
                ins = e.dma_start(out=wr_raw[:, :], in_=c.Wr[:, :])
                c.sLd.tick(ins, 16)
                ins = e.dma_start(out=c.brow[0:1, :], in_=c.rbias[None, :])
                c.sLd.tick(ins, 16)
                ins = e.memset(c.ones_b[:, :], 1.0)
                c.sMst.tick(ins)
                ins = e.memset(c.epsb[:, :], 1e-5)
                c.sMst.tick(ins)

            ldn = c.sLd.n
            ev0 = c.sWrtEv.n
            gt0 = c.sGt.n

            @B.tensor
            def _(t):
                t.wait_ge(c.sLd.h, ldn)
                t.wait_ge(c.sMst.h, 2)
                for kc in range(NKC):
                    if kc >= 2:
                        t.wait_ge(c.sWrtEv.h, ev0 + kc - 1)
                    ins = t.transpose(ppw[kc % 2][:, 0:NE],
                                      wr_raw[:, kc * 128:(kc + 1) * 128],
                                      c.eyes[0:16, 0:16])
                    c.sGt.tick(ins)

            @B.vector
            def _(v):
                for kc in range(NKC):
                    v.wait_ge(c.sGt.h, gt0 + kc + 1)
                    ins = v.tensor_copy(c.wrt[:, kc * NE:(kc + 1) * NE],
                                        ppw[kc % 2][:, 0:NE])
                    c.sWrtEv.tick(ins)
        assert c.sWrtEv.n == ev0 + NKC and c.sGt.n == gt0 + NKC


def phase_P0(c, b):
    nc = c.nc
    with contextlib.ExitStack() as st:
        xts = [c.sb(st, f"p0x{b}_{i}", [128, DM], F32) for i in range(2)]
        xnt = [c.sb(st, f"p0n{b}_{i}", [128, DM], BF16) for i in range(2)]
        sq = c.sb(st, f"p0sq{b}", [128, DM], F32)
        stt = c.sb(st, f"p0st{b}", [128, 8], F32)
        ptr = [c.psr(st, f"p0p{b}_{i}", [128, S], BF16) for i in range(2)]

        bst, bsq = c.sStat.n, c.sSq.n
        bxt, bxe = c.sXnT.n, c.sXnEv.n
        bwo, bp5 = c.sWOp.n, c.sP5Ev.n
        x_tgt = [None] * NTC

        with nc.Block(no_gpsimd_drain=True) as B:
            @B.sync
            def _(sy):
                for tc in range(NTC):
                    if tc >= 2:
                        sy.wait_ge(c.sStat.h, bst + (tc - 2) * 6 + 6)
                        sy.wait_ge(c.sSq.h, bsq + (tc - 2) * 2 + 1)
                    elif b > 0:
                        c.dOut.wait_all(sy)
                    ins = sy.dma_start(out=xts[tc % 2][:, :],
                                       in_=c.x[b, tc * 128:(tc + 1) * 128, :])
                    x_tgt[tc] = c.dX.tick(ins, tc)

            @B.tensor
            def _(t):
                if b > 0:
                    t.wait_ge(c.sP5Ev.h, bp5)
                for tc in range(NTC):
                    t.wait_ge(c.sStat.h, bst + tc * 6 + 6)
                    if tc >= 2:
                        t.wait_ge(c.sXnEv.h, bxe + tc - 1)
                    for fc in range(NKC):
                        ins = t.transpose(ptr[tc % 2][:, fc * 128:(fc + 1) * 128],
                                          xnt[tc % 2][:, fc * 128:(fc + 1) * 128],
                                          c.eyeb[:, :])
                    c.sXnT.tick(ins)

            @B.scalar
            def _(s):
                if b == 0:
                    s.wait_ge(c.sMst.h, 2)
                for tc in range(NTC):
                    c.dX.wait(s, tc, x_tgt[tc])
                    if tc >= 1:
                        s.wait_ge(c.sStat.h, bst + (tc - 1) * 6 + 4)
                    ins = s.activation(sq[:, :], xts[tc % 2][:, :], AF.Square,
                                       accum_out=stt[:, 1:2])
                    c.sSq.tick(ins)
                    s.wait_ge(c.sStat.h, bst + tc * 6 + 4)
                    ins = s.activation(stt[:, 5:6], stt[:, 4:5], AF.Sqrt,
                                       bias=c.epsb[:, :])
                    c.sSq.tick(ins)

            @B.vector
            def _(v):
                for tc in range(NTC):
                    c.dX.wait(v, tc, x_tgt[tc])
                    ins = v.reduce_sum(stt[:, 0:1], xts[tc % 2][:, :], AX.X)
                    c.sStat.tick(ins)
                    v.wait_ge(c.sStat.h, bst + tc * 6 + 1)
                    ins = v.tensor_scalar_mul(stt[:, 2:3], stt[:, 0:1], 1.0 / DM)
                    c.sStat.tick(ins)
                    v.wait_ge(c.sStat.h, bst + tc * 6 + 2)
                    ins = v.scalar_tensor_tensor(stt[:, 3:4], stt[:, 2:3], 1.0,
                                                 stt[:, 2:3], ALU.mult, ALU.mult)
                    c.sStat.tick(ins)
                    v.wait_ge(c.sStat.h, bst + tc * 6 + 3)
                    v.wait_ge(c.sSq.h, bsq + tc * 2 + 1)
                    ins = v.tensor_scalar(stt[:, 4:5], stt[:, 1:2], 1.0 / DM,
                                          stt[:, 3:4], ALU.mult, ALU.subtract)
                    c.sStat.tick(ins)
                    v.wait_ge(c.sSq.h, bsq + tc * 2 + 2)
                    ins = v.reciprocal(stt[:, 6:7], stt[:, 5:6])
                    c.sStat.tick(ins)
                    v.wait_ge(c.sStat.h, bst + tc * 6 + 5)
                    if tc >= 2:
                        v.wait_ge(c.sXnT.h, bxt + tc - 1)
                    ins = v.tensor_scalar(xnt[tc % 2][:, :], xts[tc % 2][:, :],
                                          stt[:, 2:3], stt[:, 6:7],
                                          ALU.subtract, ALU.mult)
                    c.sStat.tick(ins)
                    v.wait_ge(c.sXnT.h, bxt + tc + 1)
                    if b > 0 and tc == 0:
                        v.wait_ge(c.sWOp.h, bwo)
                    dst = c.xno.rearrange("p (kc t) -> p kc t", kc=NKC)[:, :, tc * 128:(tc + 1) * 128]
                    src = ptr[tc % 2].rearrange("p (kc t) -> p kc t", kc=NKC)
                    ins = v.tensor_copy(dst, src)
                    c.sXnEv.tick(ins)

        assert c.sStat.n == bst + NTC * 6
        assert c.sSq.n == bsq + NTC * 2
        assert c.sXnT.n == bxt + NTC
        assert c.sXnEv.n == bxe + NTC


def phase_P1(c, b):
    nc = c.nc
    with contextlib.ExitStack() as st:
        wgb = [c.sb(st, f"wg{b}_{i}", [128, NKC * 128], BF16) for i in range(2)]
        wub = [c.sb(st, f"wu{b}_{i}", [128, NKC * 128], BF16) for i in range(2)]
        wdb = [c.sb(st, f"wd{b}_{i}", [128, NIC * 128], BF16) for i in range(2)]
        gateb = c.sb(st, f"gateb{b}", [128, S], BF16)
        grow = c.sb(st, f"grow{b}", [1, S], BF16)
        silsb = [c.sb(st, f"sil{b}_{i}", [128, 512], BF16) for i in range(2)]
        h1sb = [c.sb(st, f"h1{b}_{i}", [128, 512], BF16) for i in range(2)]
        hbuf = c.sb(st, f"hbuf{b}", [128, NIC * S], BF16)
        Gtm = c.sb(st, f"Gtm{b}", [128, NTC * NE], F32)
        Gfm = c.sb(st, f"Gfm{b}", [16, S], BF16)
        L = c.sb(st, f"rL{b}", [128, NE], F32)
        Lm = c.sb(st, f"rLm{b}", [128, NE], F32)
        ex = c.sb(st, f"rex{b}", [128, NE], F32)
        exm = c.sb(st, f"rexm{b}", [128, NE], F32)
        is1 = c.sb(st, f"ris1{b}", [128, NE], F32)
        is2 = c.sb(st, f"ris2{b}", [128, NE], F32)
        G2 = c.sb(st, f"rG2{b}", [128, NE], F32)
        rr = c.sb(st, f"rr{b}", [128, 12], F32)
        pg = [c.psr(st, f"pg{b}_{i}", [128, 512], F32) for i in range(2)]
        pu = [c.psr(st, f"pu{b}_{i}", [128, 512], F32) for i in range(2)]
        pd = [c.psr(st, f"pd{b}_{i}", [128, 512], F32) for i in range(2)]
        pr = [c.psr(st, f"pr{b}_{i}", [128, 512], F32) for i in range(2)]

        xn_ready = c.sXnEv.n
        wrt_ev = c.sWrtEv.n
        bp5e, bp5t = c.sP5Ev.n, c.sP5T.n
        bLg, bRt, bRA = c.sLg.n, c.sRoute.n, c.sRtA.n
        bGt, bGE = c.sGt.n, c.sGtEv.n
        bGb, bGbE, bGwd = c.sGb.n, c.sGbEv.n, c.sGw.n
        bUG, bUU = c.sUpG.n, c.sUpU.n
        wg_tgt = [None] * (EXP * NIC)
        wu_tgt = [None] * (EXP * NIC)
        wd_tgt = [None] * (EXP * NKC)
        gw_tgt = [None] * (NE + 1)
        bSi, bH1, bHg = c.sSilu.n, c.sH1.n, c.sHg.n
        bDw, bAc = c.sDwn.n, c.sAcc.n

        elist = [(c.sWg, c.sWu, c.sWd, None)] + \
                [(c.Wg[e], c.Wu[e], c.Wd[e], e) for e in range(NE)]

        with nc.Block(no_gpsimd_drain=True) as B:
            @B.gpsimd
            def _(e):
                for ep, (dg, du, dd, gi) in enumerate(elist):
                    if gi is not None:
                        e.wait_ge(c.sGtEv.h, bGE + NTC)
                        if ep >= 2:
                            e.wait_ge(c.sGb.h, bGb + 4 * (ep - 1))
                        ins = e.dma_start(out=grow[0:1, :], in_=Gfm[gi:gi + 1, :])
                        gw_tgt[ep] = c.sGw.tick(ins, 16)
                    for ic in range(NIC):
                        gu = ep * NIC + ic
                        if gu >= 2:
                            e.wait_ge(c.sUpU.h, bUU + 4 * (gu - 1))
                        ins = e.dma_start(
                            out=wgb[gu % 2].rearrange("p (kc m) -> p kc m", kc=NKC),
                            in_=wcol(dg, ic * 128, 128))
                        wg_tgt[gu] = c.dWgA.tick(ins, gu)
                        ins = e.dma_start(
                            out=wub[gu % 2].rearrange("p (kc m) -> p kc m", kc=NKC),
                            in_=wcol(du, ic * 128, 128))
                        wu_tgt[gu] = c.dWuA.tick(ins, gu)
                    for oc in range(NKC):
                        gd = ep * NKC + oc
                        if gd >= 2:
                            e.wait_ge(c.sAcc.h, bAc + 4 * (gd - 1))
                        ins = e.dma_start(
                            out=wdb[gd % 2].rearrange("p (kc m) -> p kc m", kc=NIC),
                            in_=wcol(dd, oc * 128, 128))
                        wd_tgt[gd] = c.dWd.tick(ins, gd)

            @B.tensor
            def _(t):
                t.wait_ge(c.sXnEv.h, xn_ready)
                t.wait_ge(c.sWrtEv.h, wrt_ev)
                if b > 0:
                    t.wait_ge(c.sP5Ev.h, bp5e)
                for tc in range(NTC):
                    if tc >= 2:
                        t.wait_ge(c.sRoute.h, bRt + (tc - 2) * 17 + 1)
                    for kc in range(NKC):
                        t.matmul(pr[tc % 2][:, 0:NE],
                                 c.xn_c(kc)[:, tc * 128:(tc + 1) * 128],
                                 c.wrt[:, kc * NE:(kc + 1) * NE],
                                 start=(kc == 0), stop=False)
                    ins = t.matmul(pr[tc % 2][:, 0:NE], c.ones_b[:, :],
                                   c.brow[0:1, :], start=False, stop=True)
                    c.sLg.tick(ins)
                t.wait_ge(c.sRoute.h, bRt + NTC * 17)
                for tc in range(NTC):
                    if tc >= 2:
                        t.wait_ge(c.sGtEv.h, bGE + tc - 1)
                    ins = t.transpose(pr[tc % 2][0:16, 0:128],
                                      Gtm[:, tc * NE:(tc + 1) * NE], c.eyes[:, :])
                    c.sGt.tick(ins)
                for ep, (dg, du, dd, gi) in enumerate(elist):
                    if gi is not None:
                        ri = ep - 1
                        if ep == 1:
                            t.wait_ge(c.sGtEv.h, bGE + NTC)
                        if ep >= 2:
                            t.wait_ge(c.sHg.h, bHg + 32 * ri)
                        t.wait_ge(c.sGw.h, gw_tgt[ep])
                        for q in range(4):
                            gq = 4 * ri + q
                            if gq >= 2:
                                t.wait_ge(c.sGbEv.h, bGbE + gq - 1)
                            ins = t.matmul(pr[q % 2][:, 0:512], c.ones_b[:, :],
                                           grow[0:1, q * 512:(q + 1) * 512],
                                           start=True, stop=True)
                            c.sGb.tick(ins)
                    for ic in range(NIC):
                        gu = ep * NIC + ic
                        c.dWgA.wait(t, gu, wg_tgt[gu])
                        c.dWuA.wait(t, gu, wu_tgt[gu])
                        for ts in range(NTS):
                            grp = ep * 32 + ic * 4 + ts
                            if grp >= 2:
                                t.wait_ge(c.sSilu.h, bSi + grp - 1)
                            for kc in range(NKC):
                                ins = t.matmul(pg[grp % 2][:, :],
                                               wgb[gu % 2][:, kc * 128:(kc + 1) * 128],
                                               c.xn_c(kc)[:, ts * 512:(ts + 1) * 512],
                                               start=(kc == 0), stop=(kc == NKC - 1))
                            c.sUpG.tick(ins)
                            if grp >= 2:
                                t.wait_ge(c.sH1.h, bH1 + grp - 1)
                            for kc in range(NKC):
                                ins = t.matmul(pu[grp % 2][:, :],
                                               wub[gu % 2][:, kc * 128:(kc + 1) * 128],
                                               c.xn_c(kc)[:, ts * 512:(ts + 1) * 512],
                                               start=(kc == 0), stop=(kc == NKC - 1))
                            c.sUpU.tick(ins)
                    for oc in range(NKC):
                        gd = ep * NKC + oc
                        c.dWd.wait(t, gd, wd_tgt[gd])
                        for ts in range(NTS):
                            dgrp = ep * 64 + oc * 4 + ts
                            if dgrp >= 2:
                                t.wait_ge(c.sAcc.h, bAc + dgrp - 1)
                            if oc == 0:
                                if gi is None:
                                    t.wait_ge(c.sH1.h, bH1 + ep * 32 + 29 + ts)
                                else:
                                    t.wait_ge(c.sHg.h, bHg + 32 * (ep - 1) + 29 + ts)
                            for kc in range(NIC):
                                ins = t.matmul(
                                    pd[dgrp % 2][:, :],
                                    wdb[gd % 2][:, kc * 128:(kc + 1) * 128],
                                    hbuf[:, kc * S + ts * 512:kc * S + (ts + 1) * 512],
                                    start=(kc == 0), stop=(kc == NIC - 1))
                            c.sDwn.tick(ins)

            @B.scalar
            def _(s):
                def emit_silu(grp):
                    s.wait_ge(c.sUpG.h, bUG + grp + 1)
                    if grp >= 2:
                        s.wait_ge(c.sH1.h, bH1 + grp - 1)
                    ins = s.activation(silsb[grp % 2][:, :], pg[grp % 2][:, :],
                                       AF.Silu)
                    c.sSilu.tick(ins)

                for tc in range(NTC):
                    s.wait_ge(c.sRoute.h, bRt + tc * 17 + 4)
                    ins = s.activation(ex[:, :], L[:, :], AF.Exp,
                                       bias=rr[:, 1:2], accum_out=rr[:, 2:3])
                    c.sRtA.tick(ins)
                for sg in range(EXP * 32):
                    emit_silu(sg)

            @B.vector
            def _(v):
                for tc in range(NTC):
                    base = bRt + tc * 17
                    v.wait_ge(c.sLg.h, bLg + tc + 1)
                    if tc >= 1:
                        v.wait_ge(c.sRtA.h, bRA + tc)
                    ins = v.tensor_copy(L[:, :], pr[tc % 2][:, 0:NE])
                    c.sRoute.tick(ins)
                    steps = [
                        lambda: v.reduce_max(rr[:, 0:1], L[:, :], AX.X),
                        lambda: v.tensor_single_scalar(is1[:, :], L[:, :], rr[:, 0:1], ALU.is_ge),
                        lambda: v.tensor_scalar_mul(rr[:, 1:2], rr[:, 0:1], -1.0),
                        lambda: v.scalar_tensor_tensor(Lm[:, :], is1[:, :], -BIG, L[:, :], ALU.mult, ALU.add),
                        lambda: v.reduce_max(rr[:, 3:4], Lm[:, :], AX.X),
                        lambda: v.tensor_single_scalar(is2[:, :], Lm[:, :], rr[:, 3:4], ALU.is_ge),
                        None,
                        lambda: v.scalar_tensor_tensor(exm[:, :], is1[:, :], -4.0, ex[:, :], ALU.mult, ALU.add),
                        lambda: v.reduce_max(rr[:, 5:6], exm[:, :], AX.X),
                        lambda: v.tensor_add(rr[:, 6:7], rr[:, 4:5], rr[:, 5:6]),
                        lambda: v.scalar_tensor_tensor(rr[:, 7:8], rr[:, 2:3], 1e-9, rr[:, 6:7], ALU.mult, ALU.add),
                        lambda: v.reciprocal(rr[:, 8:9], rr[:, 7:8]),
                        lambda: v.tensor_mul(rr[:, 9:10], rr[:, 4:5], rr[:, 8:9]),
                        lambda: v.tensor_mul(rr[:, 10:11], rr[:, 5:6], rr[:, 8:9]),
                        lambda: v.tensor_scalar_mul(G2[:, :], is1[:, :], rr[:, 9:10]),
                        lambda: v.scalar_tensor_tensor(
                            Gtm[:, tc * NE:(tc + 1) * NE], is2[:, :], rr[:, 10:11],
                            G2[:, :], ALU.mult, ALU.add),
                    ]
                    for i, f in enumerate(steps):
                        v.wait_ge(c.sRoute.h, base + i + 1)
                        if f is None:
                            v.wait_ge(c.sRtA.h, bRA + tc + 1)
                            ins = v.reduce_max(rr[:, 4:5], ex[:, :], AX.X)
                        else:
                            ins = f()
                        c.sRoute.tick(ins)
                for tc in range(NTC):
                    v.wait_ge(c.sGt.h, bGt + tc + 1)
                    ins = v.tensor_copy(Gfm[0:16, tc * 128:(tc + 1) * 128],
                                        pr[tc % 2][0:16, 0:128])
                    c.sGtEv.tick(ins)
                for ep, (dg, du, dd, gi) in enumerate(elist):
                    if gi is not None:
                        for q in range(4):
                            gq = 4 * (ep - 1) + q
                            v.wait_ge(c.sGb.h, bGb + gq + 1)
                            ins = v.tensor_copy(gateb[:, q * 512:(q + 1) * 512],
                                                pr[q % 2][:, 0:512])
                            c.sGbEv.tick(ins)
                    for ic in range(NIC):
                        for ts in range(NTS):
                            grp = ep * 32 + ic * 4 + ts
                            hs = hbuf[:, ic * S + ts * 512:ic * S + (ts + 1) * 512]
                            v.wait_ge(c.sSilu.h, bSi + grp + 1)
                            v.wait_ge(c.sUpU.h, bUU + grp + 1)
                            if gi is None:
                                ins = v.tensor_mul(hs, silsb[grp % 2][:, :],
                                                   pu[grp % 2][:, :])
                                c.sH1.tick(ins)
                            else:
                                ins = v.tensor_mul(h1sb[grp % 2][:, :],
                                                   silsb[grp % 2][:, :],
                                                   pu[grp % 2][:, :])
                                c.sH1.tick(ins)
                                v.wait_ge(c.sH1.h, bH1 + grp + 1)
                                ins = v.tensor_mul(hs, h1sb[grp % 2][:, :],
                                                   gateb[:, ts * 512:(ts + 1) * 512])
                                c.sHg.tick(ins)
                    for oc in range(NKC):
                        for ts in range(NTS):
                            dgrp = ep * 64 + oc * 4 + ts
                            asl = c.acc_c(oc)[:, ts * 512:(ts + 1) * 512]
                            v.wait_ge(c.sDwn.h, bDw + dgrp + 1)
                            if ep == 0:
                                if b > 0 and dgrp == 0:
                                    v.wait_ge(c.sP5T.h, bp5t)
                                ins = v.tensor_copy(asl, pd[dgrp % 2][:, :])
                            else:
                                ins = v.tensor_tensor(asl, pd[dgrp % 2][:, :],
                                                      asl, ALU.add)
                            c.sAcc.tick(ins)

        assert c.sLg.n == bLg + NTC
        assert c.sRoute.n == bRt + NTC * 17
        assert c.sRtA.n == bRA + NTC
        assert c.sGt.n == bGt + NTC and c.sGtEv.n == bGE + NTC
        assert c.sGb.n == bGb + 64 and c.sGbEv.n == bGbE + 64
        assert c.sGw.n == bGwd + 16 * NE
        assert c.sUpG.n == bUG + EXP * 32 and c.sUpU.n == bUU + EXP * 32
        assert c.sSilu.n == bSi + EXP * 32
        assert c.sH1.n == bH1 + EXP * 32
        assert c.sHg.n == bHg + NE * 32
        assert c.sDwn.n == bDw + EXP * 64 and c.sAcc.n == bAc + EXP * 64


def phase_P2(c, b):
    nc = c.nc
    with contextlib.ExitStack() as st:
        wcb = [c.sb(st, f"wc{b}_{i}", [128, NKC * 128], BF16) for i in range(2)]
        krs = c.sb(st, f"krs{b}", [64, 512], F32)
        pab = [c.psr(st, f"pc{b}_{i}", [128, 512], F32) for i in range(2)]
        pn = c.psr(st, f"pn{b}", [128, 512], F32)
        ps2 = c.psr(st, f"ps2{b}", [128, 512], F32)

        bCp, bCe = c.sCp.n, c.sCpEv.n
        cw_tgt = [None] * 10
        bKp, bRo = c.sKp.n, c.sRope.n
        jAcc, jSil, jH1 = c.sAcc.n, c.sSilu.n, c.sH1.n
        jGbE, jGtE = c.sGbEv.n, c.sGtEv.n

        chunks = [wcol(c.W_DKV, mc * 128, 128) for mc in range(NLC)] + \
                 [wcol(c.W_DQ, mc * 128, 128) for mc in range(NLC)]

        with nc.Block(no_gpsimd_drain=True) as B:
            @B.gpsimd
            def _(e):
                for j, src in enumerate(chunks):
                    if j >= 2:
                        e.wait_ge(c.sCp.h, bCp + 4 * (j - 1))
                    ins = e.dma_start(
                        out=wcb[j % 2].rearrange("p (kc m) -> p kc m", kc=NKC),
                        in_=src)
                    cw_tgt[j] = c.dCw.tick(ins, j)
                e.wait_ge(c.sCp.h, bCp + 4 * 7)
                ins = e.dma_start(
                    out=wcb[0].rearrange("p (kc m) -> p kc m", kc=NKC)[:, :, 0:64],
                    in_=wcol(c.W_KR, 0, 64))
                cw_tgt[8] = c.dCw.tick(ins, 8)
                e.wait_ge(c.sCp.h, bCp + 4 * 8)
                ins = e.dma_start(
                    out=wcb[1].rearrange("p (kc m) -> p kc m", kc=NKC)[:, :, 0:32],
                    in_=wcol(c.W_KR, 32, 32))
                c.dCw.tick(ins, 9)
                ins = e.dma_start(
                    out=wcb[1].rearrange("p (kc m) -> p kc m", kc=NKC)[:, :, 32:64],
                    in_=wcol(c.W_KR, 0, 32))
                cw_tgt[9] = c.dCw.tick(ins, 9)

            @B.tensor
            def _(t):
                t.wait_ge(c.sAcc.h, jAcc)
                t.wait_ge(c.sSilu.h, jSil)
                t.wait_ge(c.sH1.h, jH1)
                t.wait_ge(c.sGbEv.h, jGbE)
                t.wait_ge(c.sGtEv.h, jGtE)
                for j in range(8):
                    c.dCw.wait(t, j, cw_tgt[j])
                    for ts in range(NTS):
                        r = j * 4 + ts
                        if r >= 2:
                            t.wait_ge(c.sCpEv.h, bCe + r - 1)
                        for kc in range(NKC):
                            ins = t.matmul(pab[r % 2][:, :],
                                           wcb[j % 2][:, kc * 128:(kc + 1) * 128],
                                           c.xn_c(kc)[:, ts * 512:(ts + 1) * 512],
                                           start=(kc == 0), stop=(kc == NKC - 1))
                        c.sCp.tick(ins)
                c.dCw.wait(t, 8, cw_tgt[8])
                c.dCw.wait(t, 9, cw_tgt[9])
                for ts in range(NTS):
                    if ts >= 1:
                        t.wait_ge(c.sRope.h, bRo + 3 * ts)
                    for kc in range(NKC):
                        ins = t.matmul(pn[0:64, :],
                                       wcb[0][:, kc * 128:kc * 128 + 64],
                                       c.xn_c(kc)[:, ts * 512:(ts + 1) * 512],
                                       start=(kc == 0), stop=(kc == NKC - 1))
                    c.sKp.tick(ins)
                    for kc in range(NKC):
                        ins = t.matmul(ps2[0:64, :],
                                       wcb[1][:, kc * 128:kc * 128 + 64],
                                       c.xn_c(kc)[:, ts * 512:(ts + 1) * 512],
                                       start=(kc == 0), stop=(kc == NKC - 1))
                    c.sKp.tick(ins)

            @B.vector
            def _(v):
                for j in range(8):
                    dst = c.cKV if j < NLC else c.cQ
                    mc = j % NLC
                    for ts in range(NTS):
                        r = j * 4 + ts
                        v.wait_ge(c.sCp.h, bCp + r + 1)
                        ins = v.tensor_copy(
                            dst[:, mc * S + ts * 512:mc * S + (ts + 1) * 512],
                            pab[r % 2][:, :])
                        c.sCpEv.tick(ins)
                for ts in range(NTS):
                    v.wait_ge(c.sKp.h, bKp + 2 * (ts + 1))
                    sl = slice(ts * 512, (ts + 1) * 512)
                    ins = v.tensor_mul(krs[:, :], pn[0:64, :], c.cosb[0:64, sl])
                    c.sRope.tick(ins)
                    v.wait_ge(c.sRope.h, bRo + 3 * ts + 1)
                    ins = v.tensor_mul(c.kRb[:, sl], ps2[0:64, :], c.sinb[0:64, sl])
                    c.sRope.tick(ins)
                    v.wait_ge(c.sRope.h, bRo + 3 * ts + 2)
                    ins = v.tensor_add(c.kRb[:, sl], c.kRb[:, sl], krs[:, :])
                    c.sRope.tick(ins)

        assert c.sCp.n == bCp + 32 and c.sCpEv.n == bCe + 32
        assert c.sKp.n == bKp + 8
        assert c.sRope.n == bRo + 12


def phase_P3(c, b):
    nc = c.nc
    with contextlib.ExitStack() as st:
        kbuf = c.sb(st, f"kbuf{b}", [128, S], BF16)
        qcb = c.sb(st, f"qcb{b}", [128, S], BF16)
        qrb = c.sb(st, f"qrb{b}", [64, S], BF16)
        vtm = c.sb(st, f"vtm{b}", [128, NTC * 128], BF16)
        rsc = c.sb(st, f"rsc{b}", [128, 512], F32)
        ukb = c.sb(st, f"uk{b}", [128, 512], BF16)
        uvb = c.sb(st, f"uv{b}", [128, 512], BF16)
        uqb = c.sb(st, f"uq{b}", [128, 512], BF16)
        vstage = qcb[:, 0:512]
        qrn = c.sb(st, f"qn{b}", [128, NLC * 64], BF16)
        qrs = c.sb(st, f"qs{b}", [128, NLC * 64], BF16)
        Pn = [c.sb(st, f"Pn{b}_{i}", [128, S], BF16) for i in range(2)]
        PTb = c.sb(st, f"PTb{b}", [128, NTC * 128], BF16)
        sfx = c.sb(st, f"sfx{b}", [128, 8], F32)
        Spsum = c.psr(st, f"Spsum{b}", [128, S], F32)
        pa3 = c.psr(st, f"pa3{b}", [128, 512], F32)
        pb3 = c.psr(st, f"pb3{b}", [128, 512], F32)
        pT = [c.psr(st, f"pT{b}_{i}", [128, 1024], BF16) for i in range(2)]

        bKw, bKp, bKe = c.sKw.n, c.sKp.n, c.sKpEv.n
        bS, bEx, bNm, bPn = c.sS.n, c.sExp.n, c.sNm.n, c.sPn.n
        bPT, bPV, bOe = c.sPT.n, c.sPV.n, c.sOEv.n
        bEvA, bEvD = c.sPEvA.n, c.sPEvD.n
        jCpE, jRo = c.sCpEv.n, c.sRope.n

        kp_base = [bKp + 20 * h for h in range(H + 1)]
        pt_base = [bPT + 36 * h for h in range(H + 1)]
        evA_base = [bEvA + 18 * h for h in range(H + 1)]
        evD_base = [bEvD + 18 * h for h in range(H + 1)]

        def head_dmas(e, h):
            for wb, src in ((ukb, c.W_UK), (uvb, c.W_UV), (uqb, c.W_UQ)):
                ins = e.dma_start(
                    out=wb.rearrange("p (kc m) -> p kc m", kc=NLC),
                    in_=wcol(src, h * 128, 128))
                c.sKw.tick(ins, 16)
            ins = e.dma_start(
                out=qrn.rearrange("p (kc m) -> p kc m", kc=NLC),
                in_=wcol(c.W_QR, h * 64, 64))
            c.sKw.tick(ins, 16)
            dst = qrs.rearrange("p (kc m) -> p kc m", kc=NLC)
            ins = e.dma_start(out=dst[:, :, 0:32],
                              in_=wcol(c.W_QR, h * 64 + 32, 32))
            c.sKw.tick(ins, 16)
            ins = e.dma_start(out=dst[:, :, 32:64],
                              in_=wcol(c.W_QR, h * 64, 32))
            c.sKw.tick(ins, 16)

        dma_cum = [6 * (j + 1) for j in range(H)]

        for h in range(H):
            hq = h * 16
            with nc.Block(no_gpsimd_drain=True) as B:
                @B.gpsimd
                def _(e):
                    if h == 0:
                        head_dmas(e, 0)
                    if h + 1 < H:
                        e.wait_ge(c.sKp.h, kp_base[h + 1])
                        head_dmas(e, h + 1)

                @B.tensor
                def _(t):
                    if h == 0:
                        t.wait_ge(c.sCpEv.h, jCpE)
                        t.wait_ge(c.sRope.h, jRo)
                    t.wait_ge(c.sKw.h, bKw + 16 * dma_cum[h])
                    t.wait_ge(c.sOEv.h, bOe + hq)
                    t.wait_ge(c.sKpEv.h, bKe + 12 * h)
                    for ts in range(NTS):
                        if ts >= 1:
                            t.wait_ge(c.sRope.h, jRo + 12 * h + 3 * ts)
                        for kc in range(NLC):
                            ins = t.matmul(
                                pa3[0:64, :],
                                qrn[:, kc * 64:(kc + 1) * 64],
                                c.cQ[:, kc * S + ts * 512:kc * S + (ts + 1) * 512],
                                start=(kc == 0), stop=(kc == NLC - 1))
                        c.sKp.tick(ins)
                        for kc in range(NLC):
                            ins = t.matmul(
                                pb3[0:64, :],
                                qrs[:, kc * 64:(kc + 1) * 64],
                                c.cQ[:, kc * S + ts * 512:kc * S + (ts + 1) * 512],
                                start=(kc == 0), stop=(kc == NLC - 1))
                        c.sKp.tick(ins)
                    rid = 0
                    for proj, wb, csrc in (("k", ukb, c.cKV), ("v", uvb, c.cKV),
                                           ("q", uqb, c.cQ)):
                        for ts in range(NTS):
                            if rid >= 2:
                                t.wait_ge(c.sKpEv.h, bKe + 12 * h + rid - 1)
                            elif rid == 0:
                                t.wait_ge(c.sRope.h, jRo + 12 * (h + 1))
                            bank = pa3 if rid % 2 == 0 else pb3
                            for kc in range(NLC):
                                ins = t.matmul(
                                    bank[:, :],
                                    wb[:, kc * 128:(kc + 1) * 128],
                                    csrc[:, kc * S + ts * 512:kc * S + (ts + 1) * 512],
                                    start=(kc == 0), stop=(kc == NLC - 1))
                            c.sKp.tick(ins)
                            rid += 1
                            if proj == "v":
                                t.wait_ge(c.sKpEv.h, bKe + 12 * h + rid)
                                g = ts
                                sem, eb = ((c.sPEvA, evA_base) if g % 2 == 0
                                           else (c.sPEvD, evD_base))
                                t.wait_ge(sem.h, eb[h] + g // 2)
                                for j in range(4):
                                    ins = t.transpose(pT[g % 2][:, j * 128:(j + 1) * 128],
                                                      vstage[:, j * 128:(j + 1) * 128],
                                                      c.eyeb[:, :])
                                c.sPT.tick(ins)

                    def S_mm(qt):
                        t.wait_ge(c.sExp.h, bEx + hq + qt)
                        for ns in range(NTS):
                            t.matmul(Spsum[:, ns * 512:(ns + 1) * 512],
                                     qcb[:, qt * 128:(qt + 1) * 128],
                                     kbuf[:, ns * 512:(ns + 1) * 512],
                                     start=True, stop=False)
                        for ns in range(NTS):
                            ins = t.matmul(Spsum[:, ns * 512:(ns + 1) * 512],
                                           qrb[:, qt * 128:(qt + 1) * 128],
                                           c.kRb[:, ns * 512:(ns + 1) * 512],
                                           start=False, stop=True)
                            c.sS.tick(ins)

                    t.wait_ge(c.sKpEv.h, bKe + 12 * (h + 1))
                    t.wait_ge(c.sRope.h, jRo + 12 * (h + 1))
                    S_mm(0)
                    for qt in range(16):
                        if qt < 15:
                            S_mm(qt + 1)
                        for r2 in range(2):
                            if r2 == 0:
                                t.wait_ge(c.sPn.h, bPn + 2 * (hq + qt + 1))
                            g = 4 + 2 * qt + r2
                            sem, eb = ((c.sPEvA, evA_base) if g % 2 == 0
                                       else (c.sPEvD, evD_base))
                            t.wait_ge(sem.h, eb[h] + g // 2)
                            for j in range(8):
                                tcc = r2 * 8 + j
                                ins = t.transpose(
                                    pT[g % 2][:, j * 128:(j + 1) * 128],
                                    Pn[qt % 2][:, tcc * 128:(tcc + 1) * 128],
                                    c.eyeb[:, :])
                            c.sPT.tick(ins)
                        t.wait_ge(c.sPEvA.h, evA_base[h] + qt + 3)
                        t.wait_ge(c.sPEvD.h, evD_base[h] + qt + 3)
                        t.wait_ge(c.sOEv.h, bOe + hq + qt)
                        for tcc in range(NTC):
                            ins = t.matmul(pb3[:, 0:128],
                                           vtm[:, tcc * 128:(tcc + 1) * 128],
                                           PTb[:, tcc * 128:(tcc + 1) * 128],
                                           start=(tcc == 0), stop=(tcc == NTC - 1))
                        c.sPV.tick(ins)

                @B.scalar
                def _(s):
                    rid = 0
                    if h >= 1:
                        s.wait_ge(c.sS.h, bS + 64 * hq // 16)
                    for proj in ("k", "v", "q"):
                        for ts in range(NTS):
                            s.wait_ge(c.sKp.h, kp_base[h] + 8 + rid + 1)
                            bank = pa3 if rid % 2 == 0 else pb3
                            if proj == "v" and ts >= 1:
                                s.wait_ge(c.sPT.h, pt_base[h] + ts)
                            if proj == "k":
                                ins = s.activation(kbuf[:, ts * 512:(ts + 1) * 512],
                                                   bank[:, :], AF.Copy)
                            elif proj == "q":
                                ins = s.activation(qcb[:, ts * 512:(ts + 1) * 512],
                                                   bank[:, :], AF.Copy)
                            else:
                                ins = s.activation(vstage, bank[:, :], AF.Copy)
                            c.sKpEv.tick(ins)
                            rid += 1
                            if proj == "v" and ts % 2 == 0:
                                s.wait_ge(c.sPT.h, pt_base[h] + ts + 1)
                                if h >= 1 and ts == 0:
                                    s.wait_ge(c.sPV.h, bPV + hq)
                                ins = s.activation(vtm[:, ts * 512:(ts + 1) * 512],
                                                   pT[ts % 2][:, 0:512], AF.Copy)
                                c.sPEvA.tick(ins)
                    for qt in range(16):
                        s.wait_ge(c.sNm.h, bNm + 6 * (hq + qt + 1))
                        s.wait_ge(c.sPn.h, bPn + 2 * (hq + qt))
                        if qt >= 2:
                            s.wait_ge(c.sPT.h, pt_base[h] + 4 + 2 * (qt - 1))
                        ins = s.activation(Pn[qt % 2][:, :], Spsum[:, :], AF.Exp,
                                           bias=sfx[:, 1:2], scale=ATT_SCALE,
                                           accum_out=sfx[:, 2:3])
                        c.sExp.tick(ins)
                        s.wait_ge(c.sPT.h, pt_base[h] + 4 + 2 * qt + 1)
                        s.wait_ge(c.sPV.h, bPV + hq + qt)
                        ins = s.activation(PTb[:, 0:1024], pT[0][:, :], AF.Copy)
                        c.sPEvA.tick(ins)

                @B.vector
                def _(v):
                    if h >= 1:
                        v.wait_ge(c.sS.h, bS + 64 * h)
                    for ts in range(NTS):
                        v.wait_ge(c.sKp.h, kp_base[h] + 2 * (ts + 1))
                        sl = slice(ts * 512, (ts + 1) * 512)
                        ins = v.tensor_mul(rsc[0:64, :], pa3[0:64, :], c.cosb[0:64, sl])
                        c.sRope.tick(ins)
                        v.wait_ge(c.sRope.h, jRo + 12 * h + 3 * ts + 1)
                        ins = v.tensor_mul(qrb[:, sl], pb3[0:64, :], c.sinb[0:64, sl])
                        c.sRope.tick(ins)
                        v.wait_ge(c.sRope.h, jRo + 12 * h + 3 * ts + 2)
                        ins = v.tensor_add(qrb[:, sl], qrb[:, sl], rsc[0:64, :])
                        c.sRope.tick(ins)
                    for ts in (1, 3):
                        v.wait_ge(c.sPT.h, pt_base[h] + ts + 1)
                        if h >= 1 and ts == 1:
                            v.wait_ge(c.sPV.h, bPV + hq)
                        ins = v.tensor_copy(vtm[:, ts * 512:(ts + 1) * 512],
                                            pT[1][:, 0:512])
                        c.sPEvD.tick(ins)
                    for qt in range(16):
                        qi = hq + qt
                        for ns in range(NTS):
                            v.wait_ge(c.sS.h, bS + 4 * qi + ns + 1)
                            ins = v.reduce_max(sfx[:, 4 + ns:5 + ns],
                                               Spsum[:, ns * 512:(ns + 1) * 512],
                                               AX.X)
                            c.sNm.tick(ins)
                        v.wait_ge(c.sNm.h, bNm + 6 * qi + 4)
                        ins = v.reduce_max(sfx[:, 0:1], sfx[:, 4:8], AX.X)
                        c.sNm.tick(ins)
                        v.wait_ge(c.sNm.h, bNm + 6 * qi + 5)
                        ins = v.tensor_scalar_mul(sfx[:, 1:2], sfx[:, 0:1],
                                                  -ATT_SCALE)
                        c.sNm.tick(ins)
                        if qt >= 1:
                            v.wait_ge(c.sPV.h, bPV + qi)
                            ins = v.tensor_copy(
                                c.xno[:, h * S + (qt - 1) * 128:h * S + qt * 128],
                                pb3[:, 0:128])
                            c.sOEv.tick(ins)
                        v.wait_ge(c.sExp.h, bEx + qi + 1)
                        ins = v.reciprocal(sfx[:, 3:4], sfx[:, 2:3])
                        c.sPn.tick(ins)
                        v.wait_ge(c.sPn.h, bPn + 2 * qi + 1)
                        ins = v.tensor_scalar_mul(Pn[qt % 2][:, :], Pn[qt % 2][:, :],
                                                  sfx[:, 3:4])
                        c.sPn.tick(ins)
                        v.wait_ge(c.sPT.h, pt_base[h] + 4 + 2 * qt + 2)
                        v.wait_ge(c.sPV.h, bPV + qi)
                        ins = v.tensor_copy(PTb[:, 1024:2048], pT[1][:, :])
                        c.sPEvD.tick(ins)
                    v.wait_ge(c.sPV.h, bPV + hq + 16)
                    ins = v.tensor_copy(c.xno[:, h * S + 15 * 128:h * S + 16 * 128],
                                        pb3[:, 0:128])
                    c.sOEv.tick(ins)

        assert c.sS.n == bS + H * 64
        assert c.sExp.n == bEx + H * 16
        assert c.sNm.n == bNm + 6 * H * 16
        assert c.sPn.n == bPn + 2 * H * 16
        assert c.sPV.n == bPV + H * 16 and c.sOEv.n == bOe + H * 16
        assert c.sKpEv.n == bKe + 12 * H
        assert c.sKp.n == kp_base[H]
        assert c.sPT.n == bPT + H * 36
        assert c.sPEvA.n == bEvA + 18 * H and c.sPEvD.n == bEvD + 18 * H
        assert c.sRope.n == jRo + 12 * H


def phase_P4(c, b):
    nc = c.nc
    with contextlib.ExitStack() as st:
        wob = [c.sb(st, f"wo{b}_{i}", [128, NKC * 128], BF16) for i in range(2)]
        pw4 = [c.psr(st, f"pw4{b}_{i}", [128, 512], F32) for i in range(2)]
        bP, bE = c.sWOp.n, c.sWOEv.n
        wo_tgt = [None] * NKC
        jO, jKe, jEx = c.sOEv.n, c.sKpEv.n, c.sExp.n
        jEvA, jEvD = c.sPEvA.n, c.sPEvD.n

        with nc.Block(no_gpsimd_drain=True) as B:
            @B.gpsimd
            def _(e):
                for oc in range(NKC):
                    if oc >= 2:
                        e.wait_ge(c.sWOEv.h, bE + 4 * (oc - 1))
                    ins = e.dma_start(
                        out=wob[oc % 2].rearrange("p (kc m) -> p kc m", kc=NKC),
                        in_=wcol(c.W_O, oc * 128, 128))
                    wo_tgt[oc] = c.dWO.tick(ins, oc)

            @B.tensor
            def _(t):
                t.wait_ge(c.sOEv.h, jO)
                t.wait_ge(c.sPEvA.h, jEvA)
                t.wait_ge(c.sPEvD.h, jEvD)
                t.wait_ge(c.sKpEv.h, jKe)
                t.wait_ge(c.sExp.h, jEx)
                for oc in range(NKC):
                    c.dWO.wait(t, oc, wo_tgt[oc])
                    for ts in range(NTS):
                        grp = oc * 4 + ts
                        if grp >= 2:
                            t.wait_ge(c.sWOEv.h, bE + grp - 1)
                        for kc in range(NKC):
                            ins = t.matmul(
                                pw4[grp % 2][:, :],
                                wob[oc % 2][:, kc * 128:(kc + 1) * 128],
                                c.xno[:, kc * S + ts * 512:kc * S + (ts + 1) * 512],
                                start=(kc == 0), stop=(kc == NKC - 1))
                        c.sWOp.tick(ins)

            @B.vector
            def _(v):
                for oc in range(NKC):
                    for ts in range(NTS):
                        grp = oc * 4 + ts
                        asl = c.acc_c(oc)[:, ts * 512:(ts + 1) * 512]
                        v.wait_ge(c.sWOp.h, bP + grp + 1)
                        ins = v.tensor_tensor(asl, pw4[grp % 2][:, :], asl, ALU.add)
                        c.sWOEv.tick(ins)

        assert c.sWOp.n == bP + 64 and c.sWOEv.n == bE + 64


def phase_P5(c, b):
    nc = c.nc
    with contextlib.ExitStack() as st:
        xts = [c.sb(st, f"p5x{b}_{i}", [128, DM], F32) for i in range(2)]
        ot = [c.sb(st, f"p5o{b}_{i}", [128, DM], F32) for i in range(2)]
        pt5 = [c.psr(st, f"p5p{b}_{i}", [128, S], BF16) for i in range(2)]
        bT, bEv = c.sP5T.n, c.sP5Ev.n
        jWOE = c.sWOEv.n
        x_tgt = [None] * NTC
        o_tgt = [None] * NTC

        with nc.Block(no_gpsimd_drain=True) as B:
            @B.sync
            def _(sy):
                def store(tc):
                    sy.wait_ge(c.sP5Ev.h, bEv + tc + 1)
                    ins = sy.dma_start(out=c.out[b, tc * 128:(tc + 1) * 128, :],
                                       in_=ot[tc % 2][:, :])
                    o_tgt[tc] = c.dOut.tick(ins, tc)

                for tc in range(NTC):
                    if tc >= 2:
                        sy.wait_ge(c.sP5Ev.h, bEv + tc - 1)
                    ins = sy.dma_start(out=xts[tc % 2][:, :],
                                       in_=c.x[b, tc * 128:(tc + 1) * 128, :])
                    x_tgt[tc] = c.dX.tick(ins, tc)
                    if tc >= 2:
                        store(tc - 2)
                store(NTC - 2)
                store(NTC - 1)

            @B.tensor
            def _(t):
                t.wait_ge(c.sWOEv.h, jWOE)
                for tc in range(NTC):
                    if tc >= 2:
                        t.wait_ge(c.sP5Ev.h, bEv + tc - 1)
                    for fc in range(NKC):
                        ins = t.transpose(pt5[tc % 2][:, fc * 128:(fc + 1) * 128],
                                          c.acc_c(fc)[:, tc * 128:(tc + 1) * 128],
                                          c.eyeb[:, :])
                    c.sP5T.tick(ins)

            @B.vector
            def _(v):
                for tc in range(NTC):
                    v.wait_ge(c.sP5T.h, bT + tc + 1)
                    c.dX.wait(v, tc, x_tgt[tc])
                    if tc >= 2:
                        c.dOut.wait(v, tc, o_tgt[tc - 2])
                    ins = v.tensor_tensor(ot[tc % 2][:, :], pt5[tc % 2][:, :],
                                          xts[tc % 2][:, :], ALU.add)
                    c.sP5Ev.tick(ins)

        assert c.sP5T.n == bT + NTC and c.sP5Ev.n == bEv + NTC


def make_rope_tables():
    invf = 1.0 / 10000.0 ** (np.arange(0, PD, 2, dtype=np.float64) / PD)
    t = np.arange(S, dtype=np.float64)
    ang = t[None, :] * invf[np.arange(128) % 32][:, None]
    cos = np.cos(ang).astype(np.float32)
    sin = np.sin(ang).astype(np.float32)
    sgn = np.where((np.arange(128) % 64) < 32, -1.0, 1.0).astype(np.float32)
    return cos, (sin * sgn[:, None]).astype(np.float32)


def kernel(**inputs):
    xin = np.ascontiguousarray(inputs["L0_attn_hidden_states"])
    nb = xin.shape[0] // N_CORES
    cos, sin = make_rope_tables()
    eye = np.eye(128, dtype=np.float32)
    shared = {
        "W_DKV": inputs["L0_attn_W_DKV"], "W_DQ": inputs["L0_attn_W_DQ"],
        "W_UK": inputs["L0_attn_W_UK"], "W_UV": inputs["L0_attn_W_UV"],
        "W_UQ": inputs["L0_attn_W_UQ"], "W_KR": inputs["L0_attn_W_KR"],
        "W_QR": inputs["L0_attn_W_QR"], "W_O": inputs["L0_attn_W_O"],
        "Wr": inputs["L0_ffn_Wr"], "rbias": inputs["L0_ffn_bias"],
        "Wg": inputs["L0_ffn_Wg"], "Wu": inputs["L0_ffn_Wu"],
        "Wd": inputs["L0_ffn_Wd"], "sWg": inputs["L0_ffn_sWg"],
        "sWu": inputs["L0_ffn_sWu"], "sWd": inputs["L0_ffn_sWd"],
        "eyef": eye, "ropec": cos, "ropes": sin,
    }
    shared = {k: np.ascontiguousarray(np.asarray(v, dtype=np.float32))
              for k, v in shared.items()}
    nc = build_program(nb)
    in_maps = []
    for cid in range(N_CORES):
        m = dict(shared)
        m["x"] = np.ascontiguousarray(xin[cid * nb:(cid + 1) * nb])
        in_maps.append(m)
    res = run_bass_kernel_spmd(nc, in_maps, core_ids=list(range(N_CORES)))
    return np.concatenate([np.asarray(r["out"]) for r in res.results], axis=0)


# version 04
# speedup vs baseline: 1.0690x; 1.0690x over previous
import contextlib
import math
import numpy as np
import ml_dtypes
import concourse.bass as bass
import concourse.mybir as mybir
from concourse.bass_utils import run_bass_kernel_spmd

F32 = mybir.dt.float32
BF16 = mybir.dt.bfloat16
AF = mybir.ActivationFunctionType
ALU = mybir.AluOpType
AX = mybir.AxisListType

S = 2048
DM = 2048
H, HD, LAT, PD = 16, 128, 512, 64
NE, DI = 16, 1024
NKC = DM // 128
NTC = S // 128
NTS = S // 512
NIC = DI // 128
NLC = LAT // 128
ATT_SCALE = 1.0 / math.sqrt(HD + PD)
BIG = 30000.0
N_CORES = 8
EXP = 17


class Sem:
    __slots__ = ("h", "n")

    def __init__(self, nc, name):
        self.h = nc.alloc_semaphore(name)
        self.n = 0

    def tick(self, ins, k=1):
        ins.then_inc(self.h, k)
        self.n += k
        return self.n


class DSem:

    def __init__(self, nc, name):
        self.s = [Sem(nc, name + "0"), Sem(nc, name + "1")]

    def tick(self, ins, i, k=16):
        return self.s[i % 2].tick(ins, k)

    def wait(self, eng, i, tgt):
        eng.wait_ge(self.s[i % 2].h, tgt)

    def wait_all(self, eng):
        eng.wait_ge(self.s[0].h, self.s[0].n)
        eng.wait_ge(self.s[1].h, self.s[1].n)


class Ctx:
    pass


def wcol(dram2d, c0, w):
    return dram2d[:, c0:c0 + w].rearrange("(kc p) m -> p kc m", p=128)


def build_program(nb):
    nc = bass.Bass("TRN2", target_bir_lowering=False, debug=False)
    c = Ctx()
    c.nc = nc
    c.nb = nb

    c.x = nc.dram_tensor("x", [nb, S, DM], F32, kind="ExternalInput").ap()
    for nm, sh in (("W_DKV", [DM, LAT]), ("W_DQ", [DM, LAT]), ("W_UK", [LAT, DM]),
                   ("W_UV", [LAT, DM]), ("W_UQ", [LAT, DM]), ("W_KR", [DM, PD]),
                   ("W_QR", [LAT, H * PD]), ("W_O", [DM, DM]), ("Wr", [NE, DM]),
                   ("Wg", [NE, DM, DI]), ("Wu", [NE, DM, DI]), ("Wd", [NE, DI, DM]),
                   ("sWg", [DM, DI]), ("sWu", [DM, DI]), ("sWd", [DI, DM]),
                   ("eyef", [128, 128]), ("ropec", [128, S]), ("ropes", [128, S])):
        setattr(c, nm, nc.dram_tensor(nm, sh, F32, kind="ExternalInput").ap())
    c.rbias = nc.dram_tensor("rbias", [NE], F32, kind="ExternalInput").ap()
    c.out = nc.dram_tensor("out", [nb, S, DM], F32, kind="ExternalOutput").ap()

    pes = contextlib.ExitStack()
    c.pes = pes

    def sb(stack, n, sh, d):
        return stack.enter_context(nc.sbuf_tensor(n, sh, d)).ap()

    def psr(stack, n, sh, d):
        return stack.enter_context(nc.psum_tensor(n, sh, d)).ap()

    c.sb, c.psr = sb, psr

    c.xno = sb(pes, "xno", [128, NKC * S], BF16)
    c.acc = sb(pes, "acc", [128, NKC * S], BF16)
    c.eyeb = sb(pes, "eyeb", [128, 128], BF16)
    c.eyes = sb(pes, "eyes", [128, 128], F32)
    c.cosb = sb(pes, "cosb", [128, S], BF16)
    c.sinb = sb(pes, "sinb", [128, S], BF16)
    c.wrt = sb(pes, "wrt", [128, NKC * NE], BF16)
    c.brow = sb(pes, "brow", [1, NE], BF16)
    c.ones_b = sb(pes, "ones_b", [1, 128], BF16)
    c.epsb = sb(pes, "epsb", [128, 1], F32)

    SN = ("sLd", "sStat", "sSq", "sXnT", "sXnEv",
          "sGw", "sUpG", "sUpU", "sSilu", "sH1", "sHg",
          "sDwn", "sAcc", "sGt", "sGtEv", "sRoute", "sLg", "sRtA",
          "sGb", "sGbEv", "sCp", "sCpEv", "sKp", "sRope",
          "sKw", "sKpEv", "sS", "sExp", "sNm", "sPn", "sPT", "sPTEv",
          "sPV", "sOEv", "sWOp", "sWOEv", "sP5T", "sP5Ev", "sWrtEv", "sMst", "sPEvA", "sPEvD")
    for n in SN:
        setattr(c, n, Sem(nc, n))
    c.dX = DSem(nc, "dX")
    c.dWgA = DSem(nc, "dWg")
    c.dWuA = DSem(nc, "dWu")
    c.dWd = DSem(nc, "dWd")
    c.dCw = DSem(nc, "dCw")
    c.dWO = DSem(nc, "dWO")
    c.dOut = DSem(nc, "dOut")

    c.xn_c = lambda kc: c.xno[:, kc * S:(kc + 1) * S]
    c.acc_c = lambda kc: c.acc[:, kc * S:(kc + 1) * S]

    prelude(c)
    for b in range(nb):
        phase_P0(c, b)
        phase_P1(c, b)
        with contextlib.ExitStack() as p23:
            c.cKV = sb(p23, f"cKV{b}", [128, NLC * S], BF16)
            c.cQ = sb(p23, f"cQ{b}", [128, NLC * S], BF16)
            c.kRb = sb(p23, f"kRb{b}", [64, S], BF16)
            phase_P2(c, b)
            phase_P3(c, b)
        phase_P4(c, b)
        phase_P5(c, b)

    with nc.Block(no_gpsimd_drain=True) as B:
        @B.sync
        def _(sy):
            c.dOut.wait_all(sy)

    pes.close()
    return nc


def prelude(c):
    nc = c.nc
    with contextlib.ExitStack() as st:
        ppw = [c.psr(st, f"ppw{i}", [128, 512], F32) for i in range(2)]
        wr_raw = c.sb(st, "wr_raw", [16, DM], F32)
        with nc.Block(no_gpsimd_drain=True) as B:
            @B.gpsimd
            def _(e):
                for dst, src in ((c.eyeb, c.eyef), (c.eyes, c.eyef),
                                 (c.cosb, c.ropec), (c.sinb, c.ropes)):
                    ins = e.dma_start(out=dst[:, :], in_=src[:, :])
                    c.sLd.tick(ins, 16)
                ins = e.dma_start(out=wr_raw[:, :], in_=c.Wr[:, :])
                c.sLd.tick(ins, 16)
                ins = e.dma_start(out=c.brow[0:1, :], in_=c.rbias[None, :])
                c.sLd.tick(ins, 16)
                ins = e.memset(c.ones_b[:, :], 1.0)
                c.sMst.tick(ins)
                ins = e.memset(c.epsb[:, :], 1e-5)
                c.sMst.tick(ins)

            ldn = c.sLd.n
            ev0 = c.sWrtEv.n
            gt0 = c.sGt.n

            @B.tensor
            def _(t):
                t.wait_ge(c.sLd.h, ldn)
                t.wait_ge(c.sMst.h, 2)
                for kc in range(NKC):
                    if kc >= 2:
                        t.wait_ge(c.sWrtEv.h, ev0 + kc - 1)
                    ins = t.transpose(ppw[kc % 2][:, 0:NE],
                                      wr_raw[:, kc * 128:(kc + 1) * 128],
                                      c.eyes[0:16, 0:16])
                    c.sGt.tick(ins)

            @B.vector
            def _(v):
                for kc in range(NKC):
                    v.wait_ge(c.sGt.h, gt0 + kc + 1)
                    ins = v.tensor_copy(c.wrt[:, kc * NE:(kc + 1) * NE],
                                        ppw[kc % 2][:, 0:NE])
                    c.sWrtEv.tick(ins)
        assert c.sWrtEv.n == ev0 + NKC and c.sGt.n == gt0 + NKC


def phase_P0(c, b):
    nc = c.nc
    with contextlib.ExitStack() as st:
        xts = [c.sb(st, f"p0x{b}_{i}", [128, DM], F32) for i in range(2)]
        xnt = [c.sb(st, f"p0n{b}_{i}", [128, DM], BF16) for i in range(2)]
        sq = c.sb(st, f"p0sq{b}", [128, DM], F32)
        stt = c.sb(st, f"p0st{b}", [128, 8], F32)
        ptr = [c.psr(st, f"p0p{b}_{i}", [128, S], BF16) for i in range(2)]

        bst, bsq = c.sStat.n, c.sSq.n
        bxt, bxe = c.sXnT.n, c.sXnEv.n
        bwo, bp5 = c.sWOp.n, c.sP5Ev.n
        x_tgt = [None] * NTC

        with nc.Block(no_gpsimd_drain=True) as B:
            @B.sync
            def _(sy):
                for tc in range(NTC):
                    if tc >= 2:
                        sy.wait_ge(c.sStat.h, bst + (tc - 2) * 6 + 6)
                        sy.wait_ge(c.sSq.h, bsq + (tc - 2) * 2 + 1)
                    elif b > 0:
                        c.dOut.wait_all(sy)
                    ins = sy.dma_start(out=xts[tc % 2][:, :],
                                       in_=c.x[b, tc * 128:(tc + 1) * 128, :])
                    x_tgt[tc] = c.dX.tick(ins, tc)

            @B.tensor
            def _(t):
                if b > 0:
                    t.wait_ge(c.sP5Ev.h, bp5)
                for tc in range(NTC):
                    t.wait_ge(c.sStat.h, bst + tc * 6 + 6)
                    if tc >= 2:
                        t.wait_ge(c.sXnEv.h, bxe + tc - 1)
                    for fc in range(NKC):
                        ins = t.transpose(ptr[tc % 2][:, fc * 128:(fc + 1) * 128],
                                          xnt[tc % 2][:, fc * 128:(fc + 1) * 128],
                                          c.eyeb[:, :])
                    c.sXnT.tick(ins)

            @B.scalar
            def _(s):
                if b == 0:
                    s.wait_ge(c.sMst.h, 2)
                for tc in range(NTC):
                    c.dX.wait(s, tc, x_tgt[tc])
                    if tc >= 1:
                        s.wait_ge(c.sStat.h, bst + (tc - 1) * 6 + 4)
                    ins = s.activation(sq[:, :], xts[tc % 2][:, :], AF.Square,
                                       accum_out=stt[:, 1:2])
                    c.sSq.tick(ins)
                    s.wait_ge(c.sStat.h, bst + tc * 6 + 4)
                    ins = s.activation(stt[:, 5:6], stt[:, 4:5], AF.Sqrt,
                                       bias=c.epsb[:, :])
                    c.sSq.tick(ins)

            @B.vector
            def _(v):
                for tc in range(NTC):
                    c.dX.wait(v, tc, x_tgt[tc])
                    ins = v.reduce_sum(stt[:, 0:1], xts[tc % 2][:, :], AX.X)
                    c.sStat.tick(ins)
                    v.wait_ge(c.sStat.h, bst + tc * 6 + 1)
                    ins = v.tensor_scalar_mul(stt[:, 2:3], stt[:, 0:1], 1.0 / DM)
                    c.sStat.tick(ins)
                    v.wait_ge(c.sStat.h, bst + tc * 6 + 2)
                    ins = v.scalar_tensor_tensor(stt[:, 3:4], stt[:, 2:3], 1.0,
                                                 stt[:, 2:3], ALU.mult, ALU.mult)
                    c.sStat.tick(ins)
                    v.wait_ge(c.sStat.h, bst + tc * 6 + 3)
                    v.wait_ge(c.sSq.h, bsq + tc * 2 + 1)
                    ins = v.tensor_scalar(stt[:, 4:5], stt[:, 1:2], 1.0 / DM,
                                          stt[:, 3:4], ALU.mult, ALU.subtract)
                    c.sStat.tick(ins)
                    v.wait_ge(c.sSq.h, bsq + tc * 2 + 2)
                    ins = v.reciprocal(stt[:, 6:7], stt[:, 5:6])
                    c.sStat.tick(ins)
                    v.wait_ge(c.sStat.h, bst + tc * 6 + 5)
                    if tc >= 2:
                        v.wait_ge(c.sXnT.h, bxt + tc - 1)
                    ins = v.tensor_scalar(xnt[tc % 2][:, :], xts[tc % 2][:, :],
                                          stt[:, 2:3], stt[:, 6:7],
                                          ALU.subtract, ALU.mult)
                    c.sStat.tick(ins)
                    v.wait_ge(c.sXnT.h, bxt + tc + 1)
                    if b > 0 and tc == 0:
                        v.wait_ge(c.sWOp.h, bwo)
                    dst = c.xno.rearrange("p (kc t) -> p kc t", kc=NKC)[:, :, tc * 128:(tc + 1) * 128]
                    src = ptr[tc % 2].rearrange("p (kc t) -> p kc t", kc=NKC)
                    ins = v.tensor_copy(dst, src)
                    c.sXnEv.tick(ins)

        assert c.sStat.n == bst + NTC * 6
        assert c.sSq.n == bsq + NTC * 2
        assert c.sXnT.n == bxt + NTC
        assert c.sXnEv.n == bxe + NTC


def phase_P1(c, b):
    nc = c.nc
    with contextlib.ExitStack() as st:
        wgb = [c.sb(st, f"wg{b}_{i}", [128, NKC * 128], BF16) for i in range(2)]
        wub = [c.sb(st, f"wu{b}_{i}", [128, NKC * 128], BF16) for i in range(2)]
        wdb = [c.sb(st, f"wd{b}_{i}", [128, NIC * 128], BF16) for i in range(2)]
        gateb = c.sb(st, f"gateb{b}", [128, S], BF16)
        grow = c.sb(st, f"grow{b}", [1, S], BF16)
        silsb = [c.sb(st, f"sil{b}_{i}", [128, 512], BF16) for i in range(2)]
        h1sb = [c.sb(st, f"h1{b}_{i}", [128, 512], BF16) for i in range(2)]
        hbuf = c.sb(st, f"hbuf{b}", [128, NIC * S], BF16)
        Gtm = c.sb(st, f"Gtm{b}", [128, NTC * NE], F32)
        Gfm = c.sb(st, f"Gfm{b}", [16, S], BF16)
        L = c.sb(st, f"rL{b}", [128, NE], F32)
        Lm = c.sb(st, f"rLm{b}", [128, NE], F32)
        ex = c.sb(st, f"rex{b}", [128, NE], F32)
        exm = c.sb(st, f"rexm{b}", [128, NE], F32)
        is1 = c.sb(st, f"ris1{b}", [128, NE], F32)
        is2 = c.sb(st, f"ris2{b}", [128, NE], F32)
        G2 = c.sb(st, f"rG2{b}", [128, NE], F32)
        rr = c.sb(st, f"rr{b}", [128, 12], F32)
        pg = [c.psr(st, f"pg{b}_{i}", [128, 512], F32) for i in range(2)]
        pu = [c.psr(st, f"pu{b}_{i}", [128, 512], F32) for i in range(2)]
        pd = [c.psr(st, f"pd{b}_{i}", [128, 512], F32) for i in range(2)]
        pr = [c.psr(st, f"pr{b}_{i}", [128, 512], F32) for i in range(2)]

        xn_ready = c.sXnEv.n
        wrt_ev = c.sWrtEv.n
        bp5e, bp5t = c.sP5Ev.n, c.sP5T.n
        bLg, bRt, bRA = c.sLg.n, c.sRoute.n, c.sRtA.n
        bGt, bGE = c.sGt.n, c.sGtEv.n
        bGb, bGbE, bGwd = c.sGb.n, c.sGbEv.n, c.sGw.n
        bUG, bUU = c.sUpG.n, c.sUpU.n
        wg_tgt = [None] * (EXP * NIC)
        wu_tgt = [None] * (EXP * NIC)
        wd_tgt = [None] * (EXP * NKC)
        gw_tgt = [None] * (NE + 1)
        bSi, bH1, bHg = c.sSilu.n, c.sH1.n, c.sHg.n
        bDw, bAc = c.sDwn.n, c.sAcc.n

        elist = [(c.sWg, c.sWu, c.sWd, None)] + \
                [(c.Wg[e], c.Wu[e], c.Wd[e], e) for e in range(NE)]

        with nc.Block(no_gpsimd_drain=True) as B:
            @B.gpsimd
            def _(e):
                for ep, (dg, du, dd, gi) in enumerate(elist):
                    if gi is not None:
                        e.wait_ge(c.sGtEv.h, bGE + NTC)
                        if ep >= 2:
                            e.wait_ge(c.sGb.h, bGb + 4 * (ep - 1))
                        ins = e.dma_start(out=grow[0:1, :], in_=Gfm[gi:gi + 1, :])
                        gw_tgt[ep] = c.sGw.tick(ins, 16)
                    for ic in range(NIC):
                        gu = ep * NIC + ic
                        if gu >= 2:
                            e.wait_ge(c.sUpU.h, bUU + 4 * (gu - 1))
                        ins = e.dma_start(
                            out=wgb[gu % 2].rearrange("p (kc m) -> p kc m", kc=NKC),
                            in_=wcol(dg, ic * 128, 128))
                        wg_tgt[gu] = c.dWgA.tick(ins, gu)
                        ins = e.dma_start(
                            out=wub[gu % 2].rearrange("p (kc m) -> p kc m", kc=NKC),
                            in_=wcol(du, ic * 128, 128))
                        wu_tgt[gu] = c.dWuA.tick(ins, gu)
                    for oc in range(NKC):
                        gd = ep * NKC + oc
                        if gd >= 2:
                            e.wait_ge(c.sAcc.h, bAc + 4 * (gd - 1))
                        ins = e.dma_start(
                            out=wdb[gd % 2].rearrange("p (kc m) -> p kc m", kc=NIC),
                            in_=wcol(dd, oc * 128, 128))
                        wd_tgt[gd] = c.dWd.tick(ins, gd)

            @B.tensor
            def _(t):
                t.wait_ge(c.sXnEv.h, xn_ready)
                t.wait_ge(c.sWrtEv.h, wrt_ev)
                if b > 0:
                    t.wait_ge(c.sP5Ev.h, bp5e)
                for tc in range(NTC):
                    if tc >= 2:
                        t.wait_ge(c.sRoute.h, bRt + (tc - 2) * 17 + 1)
                    for kc in range(NKC):
                        t.matmul(pr[tc % 2][:, 0:NE],
                                 c.xn_c(kc)[:, tc * 128:(tc + 1) * 128],
                                 c.wrt[:, kc * NE:(kc + 1) * NE],
                                 start=(kc == 0), stop=False)
                    ins = t.matmul(pr[tc % 2][:, 0:NE], c.ones_b[:, :],
                                   c.brow[0:1, :], start=False, stop=True)
                    c.sLg.tick(ins)
                t.wait_ge(c.sRoute.h, bRt + NTC * 17)
                for tc in range(NTC):
                    if tc >= 2:
                        t.wait_ge(c.sGtEv.h, bGE + tc - 1)
                    ins = t.transpose(pr[tc % 2][0:16, 0:128],
                                      Gtm[:, tc * NE:(tc + 1) * NE], c.eyes[:, :])
                    c.sGt.tick(ins)
                for ep, (dg, du, dd, gi) in enumerate(elist):
                    if gi is not None:
                        ri = ep - 1
                        if ep == 1:
                            t.wait_ge(c.sGtEv.h, bGE + NTC)
                        if ep >= 2:
                            t.wait_ge(c.sHg.h, bHg + 32 * ri)
                        t.wait_ge(c.sGw.h, gw_tgt[ep])
                        for q in range(4):
                            gq = 4 * ri + q
                            if gq >= 2:
                                t.wait_ge(c.sGbEv.h, bGbE + gq - 1)
                            ins = t.matmul(pr[q % 2][:, 0:512], c.ones_b[:, :],
                                           grow[0:1, q * 512:(q + 1) * 512],
                                           start=True, stop=True)
                            c.sGb.tick(ins)
                    for ic in range(NIC):
                        gu = ep * NIC + ic
                        c.dWgA.wait(t, gu, wg_tgt[gu])
                        c.dWuA.wait(t, gu, wu_tgt[gu])
                        for ts in range(NTS):
                            grp = ep * 32 + ic * 4 + ts
                            if grp >= 2:
                                t.wait_ge(c.sSilu.h, bSi + grp - 1)
                            for kc in range(NKC):
                                ins = t.matmul(pg[grp % 2][:, :],
                                               wgb[gu % 2][:, kc * 128:(kc + 1) * 128],
                                               c.xn_c(kc)[:, ts * 512:(ts + 1) * 512],
                                               start=(kc == 0), stop=(kc == NKC - 1))
                            c.sUpG.tick(ins)
                            if grp >= 2:
                                t.wait_ge(c.sH1.h, bH1 + grp - 1)
                            for kc in range(NKC):
                                ins = t.matmul(pu[grp % 2][:, :],
                                               wub[gu % 2][:, kc * 128:(kc + 1) * 128],
                                               c.xn_c(kc)[:, ts * 512:(ts + 1) * 512],
                                               start=(kc == 0), stop=(kc == NKC - 1))
                            c.sUpU.tick(ins)
                    for oc in range(NKC):
                        gd = ep * NKC + oc
                        c.dWd.wait(t, gd, wd_tgt[gd])
                        for ts in range(NTS):
                            dgrp = ep * 64 + oc * 4 + ts
                            if dgrp >= 2:
                                t.wait_ge(c.sAcc.h, bAc + dgrp - 1)
                            if oc == 0:
                                if gi is None:
                                    t.wait_ge(c.sH1.h, bH1 + ep * 32 + 29 + ts)
                                else:
                                    t.wait_ge(c.sHg.h, bHg + 32 * (ep - 1) + 29 + ts)
                            for kc in range(NIC):
                                ins = t.matmul(
                                    pd[dgrp % 2][:, :],
                                    wdb[gd % 2][:, kc * 128:(kc + 1) * 128],
                                    hbuf[:, kc * S + ts * 512:kc * S + (ts + 1) * 512],
                                    start=(kc == 0), stop=(kc == NIC - 1))
                            c.sDwn.tick(ins)

            @B.scalar
            def _(s):
                def emit_silu(grp):
                    s.wait_ge(c.sUpG.h, bUG + grp + 1)
                    if grp >= 2:
                        s.wait_ge(c.sH1.h, bH1 + grp - 1)
                    ins = s.activation(silsb[grp % 2][:, :], pg[grp % 2][:, :],
                                       AF.Silu)
                    c.sSilu.tick(ins)

                for tc in range(NTC):
                    s.wait_ge(c.sRoute.h, bRt + tc * 17 + 4)
                    ins = s.activation(ex[:, :], L[:, :], AF.Exp,
                                       bias=rr[:, 1:2], accum_out=rr[:, 2:3])
                    c.sRtA.tick(ins)
                for sg in range(EXP * 32):
                    emit_silu(sg)

            @B.vector
            def _(v):
                for tc in range(NTC):
                    base = bRt + tc * 17
                    v.wait_ge(c.sLg.h, bLg + tc + 1)
                    if tc >= 1:
                        v.wait_ge(c.sRtA.h, bRA + tc)
                    ins = v.tensor_copy(L[:, :], pr[tc % 2][:, 0:NE])
                    c.sRoute.tick(ins)
                    steps = [
                        lambda: v.reduce_max(rr[:, 0:1], L[:, :], AX.X),
                        lambda: v.tensor_single_scalar(is1[:, :], L[:, :], rr[:, 0:1], ALU.is_ge),
                        lambda: v.tensor_scalar_mul(rr[:, 1:2], rr[:, 0:1], -1.0),
                        lambda: v.scalar_tensor_tensor(Lm[:, :], is1[:, :], -BIG, L[:, :], ALU.mult, ALU.add),
                        lambda: v.reduce_max(rr[:, 3:4], Lm[:, :], AX.X),
                        lambda: v.tensor_single_scalar(is2[:, :], Lm[:, :], rr[:, 3:4], ALU.is_ge),
                        None,
                        lambda: v.scalar_tensor_tensor(exm[:, :], is1[:, :], -4.0, ex[:, :], ALU.mult, ALU.add),
                        lambda: v.reduce_max(rr[:, 5:6], exm[:, :], AX.X),
                        lambda: v.tensor_add(rr[:, 6:7], rr[:, 4:5], rr[:, 5:6]),
                        lambda: v.scalar_tensor_tensor(rr[:, 7:8], rr[:, 2:3], 1e-9, rr[:, 6:7], ALU.mult, ALU.add),
                        lambda: v.reciprocal(rr[:, 8:9], rr[:, 7:8]),
                        lambda: v.tensor_mul(rr[:, 9:10], rr[:, 4:5], rr[:, 8:9]),
                        lambda: v.tensor_mul(rr[:, 10:11], rr[:, 5:6], rr[:, 8:9]),
                        lambda: v.tensor_scalar_mul(G2[:, :], is1[:, :], rr[:, 9:10]),
                        lambda: v.scalar_tensor_tensor(
                            Gtm[:, tc * NE:(tc + 1) * NE], is2[:, :], rr[:, 10:11],
                            G2[:, :], ALU.mult, ALU.add),
                    ]
                    for i, f in enumerate(steps):
                        v.wait_ge(c.sRoute.h, base + i + 1)
                        if f is None:
                            v.wait_ge(c.sRtA.h, bRA + tc + 1)
                            ins = v.reduce_max(rr[:, 4:5], ex[:, :], AX.X)
                        else:
                            ins = f()
                        c.sRoute.tick(ins)
                for tc in range(NTC):
                    v.wait_ge(c.sGt.h, bGt + tc + 1)
                    ins = v.tensor_copy(Gfm[0:16, tc * 128:(tc + 1) * 128],
                                        pr[tc % 2][0:16, 0:128])
                    c.sGtEv.tick(ins)
                for ep, (dg, du, dd, gi) in enumerate(elist):
                    if gi is not None:
                        for q in range(4):
                            gq = 4 * (ep - 1) + q
                            v.wait_ge(c.sGb.h, bGb + gq + 1)
                            ins = v.tensor_copy(gateb[:, q * 512:(q + 1) * 512],
                                                pr[q % 2][:, 0:512])
                            c.sGbEv.tick(ins)
                    for ic in range(NIC):
                        for ts in range(NTS):
                            grp = ep * 32 + ic * 4 + ts
                            hs = hbuf[:, ic * S + ts * 512:ic * S + (ts + 1) * 512]
                            v.wait_ge(c.sSilu.h, bSi + grp + 1)
                            v.wait_ge(c.sUpU.h, bUU + grp + 1)
                            if gi is None:
                                ins = v.tensor_mul(hs, silsb[grp % 2][:, :],
                                                   pu[grp % 2][:, :])
                                c.sH1.tick(ins)
                            else:
                                ins = v.tensor_mul(h1sb[grp % 2][:, :],
                                                   silsb[grp % 2][:, :],
                                                   pu[grp % 2][:, :])
                                c.sH1.tick(ins)
                                v.wait_ge(c.sH1.h, bH1 + grp + 1)
                                ins = v.tensor_mul(hs, h1sb[grp % 2][:, :],
                                                   gateb[:, ts * 512:(ts + 1) * 512])
                                c.sHg.tick(ins)
                    for oc in range(NKC):
                        for ts in range(NTS):
                            dgrp = ep * 64 + oc * 4 + ts
                            asl = c.acc_c(oc)[:, ts * 512:(ts + 1) * 512]
                            v.wait_ge(c.sDwn.h, bDw + dgrp + 1)
                            if ep == 0:
                                if b > 0 and dgrp == 0:
                                    v.wait_ge(c.sP5T.h, bp5t)
                                ins = v.tensor_copy(asl, pd[dgrp % 2][:, :])
                            else:
                                ins = v.tensor_tensor(asl, pd[dgrp % 2][:, :],
                                                      asl, ALU.add)
                            c.sAcc.tick(ins)

        assert c.sLg.n == bLg + NTC
        assert c.sRoute.n == bRt + NTC * 17
        assert c.sRtA.n == bRA + NTC
        assert c.sGt.n == bGt + NTC and c.sGtEv.n == bGE + NTC
        assert c.sGb.n == bGb + 64 and c.sGbEv.n == bGbE + 64
        assert c.sGw.n == bGwd + 16 * NE
        assert c.sUpG.n == bUG + EXP * 32 and c.sUpU.n == bUU + EXP * 32
        assert c.sSilu.n == bSi + EXP * 32
        assert c.sH1.n == bH1 + EXP * 32
        assert c.sHg.n == bHg + NE * 32
        assert c.sDwn.n == bDw + EXP * 64 and c.sAcc.n == bAc + EXP * 64


def phase_P2(c, b):
    nc = c.nc
    with contextlib.ExitStack() as st:
        wcb = [c.sb(st, f"wc{b}_{i}", [128, NKC * 128], BF16) for i in range(2)]
        krs = c.sb(st, f"krs{b}", [64, 512], F32)
        pab = [c.psr(st, f"pc{b}_{i}", [128, 512], F32) for i in range(2)]
        pn = c.psr(st, f"pn{b}", [128, 512], F32)
        ps2 = c.psr(st, f"ps2{b}", [128, 512], F32)

        bCp, bCe = c.sCp.n, c.sCpEv.n
        cw_tgt = [None] * 10
        bKp, bRo = c.sKp.n, c.sRope.n
        jAcc, jSil, jH1 = c.sAcc.n, c.sSilu.n, c.sH1.n
        jGbE, jGtE = c.sGbEv.n, c.sGtEv.n

        chunks = [wcol(c.W_DKV, mc * 128, 128) for mc in range(NLC)] + \
                 [wcol(c.W_DQ, mc * 128, 128) for mc in range(NLC)]

        with nc.Block(no_gpsimd_drain=True) as B:
            @B.gpsimd
            def _(e):
                for j, src in enumerate(chunks):
                    if j >= 2:
                        e.wait_ge(c.sCp.h, bCp + 4 * (j - 1))
                    ins = e.dma_start(
                        out=wcb[j % 2].rearrange("p (kc m) -> p kc m", kc=NKC),
                        in_=src)
                    cw_tgt[j] = c.dCw.tick(ins, j)
                e.wait_ge(c.sCp.h, bCp + 4 * 7)
                ins = e.dma_start(
                    out=wcb[0].rearrange("p (kc m) -> p kc m", kc=NKC)[:, :, 0:64],
                    in_=wcol(c.W_KR, 0, 64))
                cw_tgt[8] = c.dCw.tick(ins, 8)
                e.wait_ge(c.sCp.h, bCp + 4 * 8)
                ins = e.dma_start(
                    out=wcb[1].rearrange("p (kc m) -> p kc m", kc=NKC)[:, :, 0:32],
                    in_=wcol(c.W_KR, 32, 32))
                c.dCw.tick(ins, 9)
                ins = e.dma_start(
                    out=wcb[1].rearrange("p (kc m) -> p kc m", kc=NKC)[:, :, 32:64],
                    in_=wcol(c.W_KR, 0, 32))
                cw_tgt[9] = c.dCw.tick(ins, 9)

            @B.tensor
            def _(t):
                t.wait_ge(c.sAcc.h, jAcc)
                t.wait_ge(c.sSilu.h, jSil)
                t.wait_ge(c.sH1.h, jH1)
                t.wait_ge(c.sGbEv.h, jGbE)
                t.wait_ge(c.sGtEv.h, jGtE)
                for j in range(8):
                    c.dCw.wait(t, j, cw_tgt[j])
                    for ts in range(NTS):
                        r = j * 4 + ts
                        if r >= 2:
                            t.wait_ge(c.sCpEv.h, bCe + r - 1)
                        for kc in range(NKC):
                            ins = t.matmul(pab[r % 2][:, :],
                                           wcb[j % 2][:, kc * 128:(kc + 1) * 128],
                                           c.xn_c(kc)[:, ts * 512:(ts + 1) * 512],
                                           start=(kc == 0), stop=(kc == NKC - 1))
                        c.sCp.tick(ins)
                c.dCw.wait(t, 8, cw_tgt[8])
                c.dCw.wait(t, 9, cw_tgt[9])
                for ts in range(NTS):
                    if ts >= 1:
                        t.wait_ge(c.sRope.h, bRo + 3 * ts)
                    for kc in range(NKC):
                        ins = t.matmul(pn[0:64, :],
                                       wcb[0][:, kc * 128:kc * 128 + 64],
                                       c.xn_c(kc)[:, ts * 512:(ts + 1) * 512],
                                       start=(kc == 0), stop=(kc == NKC - 1))
                    c.sKp.tick(ins)
                    for kc in range(NKC):
                        ins = t.matmul(ps2[0:64, :],
                                       wcb[1][:, kc * 128:kc * 128 + 64],
                                       c.xn_c(kc)[:, ts * 512:(ts + 1) * 512],
                                       start=(kc == 0), stop=(kc == NKC - 1))
                    c.sKp.tick(ins)

            @B.vector
            def _(v):
                for j in range(8):
                    dst = c.cKV if j < NLC else c.cQ
                    mc = j % NLC
                    for ts in range(NTS):
                        r = j * 4 + ts
                        v.wait_ge(c.sCp.h, bCp + r + 1)
                        ins = v.tensor_copy(
                            dst[:, mc * S + ts * 512:mc * S + (ts + 1) * 512],
                            pab[r % 2][:, :])
                        c.sCpEv.tick(ins)
                for ts in range(NTS):
                    v.wait_ge(c.sKp.h, bKp + 2 * (ts + 1))
                    sl = slice(ts * 512, (ts + 1) * 512)
                    ins = v.tensor_mul(krs[:, :], pn[0:64, :], c.cosb[0:64, sl])
                    c.sRope.tick(ins)
                    v.wait_ge(c.sRope.h, bRo + 3 * ts + 1)
                    ins = v.tensor_mul(c.kRb[:, sl], ps2[0:64, :], c.sinb[0:64, sl])
                    c.sRope.tick(ins)
                    v.wait_ge(c.sRope.h, bRo + 3 * ts + 2)
                    ins = v.tensor_add(c.kRb[:, sl], c.kRb[:, sl], krs[:, :])
                    c.sRope.tick(ins)

        assert c.sCp.n == bCp + 32 and c.sCpEv.n == bCe + 32
        assert c.sKp.n == bKp + 8
        assert c.sRope.n == bRo + 12


def phase_P3(c, b):
    nc = c.nc
    with contextlib.ExitStack() as st:
        kbuf = c.sb(st, f"kbuf{b}", [128, S], BF16)
        qcb = c.sb(st, f"qcb{b}", [128, S], BF16)
        qrb = c.sb(st, f"qrb{b}", [64, S], BF16)
        vtm = c.sb(st, f"vtm{b}", [128, NTC * 128], BF16)
        rsc = c.sb(st, f"rsc{b}", [128, 512], F32)
        ukb = c.sb(st, f"uk{b}", [128, 512], BF16)
        uvb = c.sb(st, f"uv{b}", [128, 512], BF16)
        uqb = c.sb(st, f"uq{b}", [128, 512], BF16)
        vstage = qcb[:, 0:512]
        qrn = c.sb(st, f"qn{b}", [128, NLC * 64], BF16)
        qrs = c.sb(st, f"qs{b}", [128, NLC * 64], BF16)
        Pn = [c.sb(st, f"Pn{b}_{i}", [128, S], BF16) for i in range(2)]
        PTb = c.sb(st, f"PTb{b}", [128, NTC * 128], BF16)
        sfx = c.sb(st, f"sfx{b}", [128, 16], F32)
        Spsum = c.psr(st, f"Spsum{b}", [128, S], F32)
        pa3 = c.psr(st, f"pa3{b}", [128, 512], F32)
        pb3 = c.psr(st, f"pb3{b}", [128, 512], F32)
        pT = [c.psr(st, f"pT{b}_{i}", [128, 1024], BF16) for i in range(2)]

        bKw, bKp, bKe = c.sKw.n, c.sKp.n, c.sKpEv.n
        bS, bEx, bNm, bPn = c.sS.n, c.sExp.n, c.sNm.n, c.sPn.n
        bPT, bPV, bOe = c.sPT.n, c.sPV.n, c.sOEv.n
        bEvA, bEvD = c.sPEvA.n, c.sPEvD.n
        jCpE, jRo = c.sCpEv.n, c.sRope.n

        kp_base = [bKp + 20 * h for h in range(H + 1)]
        pt_base = [bPT + 36 * h for h in range(H + 1)]
        evA_base = [bEvA + 18 * h for h in range(H + 1)]
        evD_base = [bEvD + 18 * h for h in range(H + 1)]

        def head_dmas(e, h):
            for wb, src in ((ukb, c.W_UK), (uvb, c.W_UV), (uqb, c.W_UQ)):
                ins = e.dma_start(
                    out=wb.rearrange("p (kc m) -> p kc m", kc=NLC),
                    in_=wcol(src, h * 128, 128))
                c.sKw.tick(ins, 16)
            ins = e.dma_start(
                out=qrn.rearrange("p (kc m) -> p kc m", kc=NLC),
                in_=wcol(c.W_QR, h * 64, 64))
            c.sKw.tick(ins, 16)
            dst = qrs.rearrange("p (kc m) -> p kc m", kc=NLC)
            ins = e.dma_start(out=dst[:, :, 0:32],
                              in_=wcol(c.W_QR, h * 64 + 32, 32))
            c.sKw.tick(ins, 16)
            ins = e.dma_start(out=dst[:, :, 32:64],
                              in_=wcol(c.W_QR, h * 64, 32))
            c.sKw.tick(ins, 16)

        dma_cum = [6 * (j + 1) for j in range(H)]

        for h in range(H):
            hq = h * 16
            with nc.Block(no_gpsimd_drain=True) as B:
                @B.gpsimd
                def _(e):
                    if h == 0:
                        head_dmas(e, 0)
                    if h + 1 < H:
                        e.wait_ge(c.sKp.h, kp_base[h + 1])
                        head_dmas(e, h + 1)

                @B.tensor
                def _(t):
                    if h == 0:
                        t.wait_ge(c.sCpEv.h, jCpE)
                        t.wait_ge(c.sRope.h, jRo)
                    t.wait_ge(c.sKw.h, bKw + 16 * dma_cum[h])
                    t.wait_ge(c.sOEv.h, bOe + hq)
                    t.wait_ge(c.sKpEv.h, bKe + 12 * h)
                    for ts in range(NTS):
                        if ts >= 1:
                            t.wait_ge(c.sRope.h, jRo + 12 * h + 3 * ts)
                        for kc in range(NLC):
                            ins = t.matmul(
                                pa3[0:64, :],
                                qrn[:, kc * 64:(kc + 1) * 64],
                                c.cQ[:, kc * S + ts * 512:kc * S + (ts + 1) * 512],
                                start=(kc == 0), stop=(kc == NLC - 1))
                        c.sKp.tick(ins)
                        for kc in range(NLC):
                            ins = t.matmul(
                                pb3[0:64, :],
                                qrs[:, kc * 64:(kc + 1) * 64],
                                c.cQ[:, kc * S + ts * 512:kc * S + (ts + 1) * 512],
                                start=(kc == 0), stop=(kc == NLC - 1))
                        c.sKp.tick(ins)
                    rid = 0
                    for proj, wb, csrc in (("k", ukb, c.cKV), ("v", uvb, c.cKV),
                                           ("q", uqb, c.cQ)):
                        for ts in range(NTS):
                            if rid >= 2:
                                t.wait_ge(c.sKpEv.h, bKe + 12 * h + rid - 1)
                            elif rid == 0:
                                t.wait_ge(c.sRope.h, jRo + 12 * (h + 1))
                            bank = pa3 if rid % 2 == 0 else pb3
                            for kc in range(NLC):
                                ins = t.matmul(
                                    bank[:, :],
                                    wb[:, kc * 128:(kc + 1) * 128],
                                    csrc[:, kc * S + ts * 512:kc * S + (ts + 1) * 512],
                                    start=(kc == 0), stop=(kc == NLC - 1))
                            c.sKp.tick(ins)
                            rid += 1
                            if proj == "v":
                                t.wait_ge(c.sKpEv.h, bKe + 12 * h + rid)
                                g = ts
                                sem, eb = ((c.sPEvA, evA_base) if g % 2 == 0
                                           else (c.sPEvD, evD_base))
                                t.wait_ge(sem.h, eb[h] + g // 2)
                                for j in range(4):
                                    ins = t.transpose(pT[g % 2][:, j * 128:(j + 1) * 128],
                                                      vstage[:, j * 128:(j + 1) * 128],
                                                      c.eyeb[:, :])
                                c.sPT.tick(ins)

                    def S_mm(qt):
                        for ns in range(NTS):
                            if qt > 0:
                                t.wait_ge(c.sExp.h, bEx + 4 * (hq + qt - 1) + ns + 1)
                            t.matmul(Spsum[:, ns * 512:(ns + 1) * 512],
                                     qcb[:, qt * 128:(qt + 1) * 128],
                                     kbuf[:, ns * 512:(ns + 1) * 512],
                                     start=True, stop=False)
                        for ns in range(NTS):
                            ins = t.matmul(Spsum[:, ns * 512:(ns + 1) * 512],
                                           qrb[:, qt * 128:(qt + 1) * 128],
                                           c.kRb[:, ns * 512:(ns + 1) * 512],
                                           start=False, stop=True)
                            c.sS.tick(ins)

                    t.wait_ge(c.sKpEv.h, bKe + 12 * (h + 1))
                    t.wait_ge(c.sRope.h, jRo + 12 * (h + 1))
                    t.wait_ge(c.sExp.h, bEx + 4 * hq)
                    S_mm(0)
                    for qt in range(16):
                        if qt < 15:
                            S_mm(qt + 1)
                        for r2 in range(2):
                            if r2 == 0:
                                t.wait_ge(c.sPn.h, bPn + 3 * (hq + qt + 1))
                            g = 4 + 2 * qt + r2
                            sem, eb = ((c.sPEvA, evA_base) if g % 2 == 0
                                       else (c.sPEvD, evD_base))
                            t.wait_ge(sem.h, eb[h] + g // 2)
                            for j in range(8):
                                tcc = r2 * 8 + j
                                ins = t.transpose(
                                    pT[g % 2][:, j * 128:(j + 1) * 128],
                                    Pn[qt % 2][:, tcc * 128:(tcc + 1) * 128],
                                    c.eyeb[:, :])
                            c.sPT.tick(ins)
                        t.wait_ge(c.sPEvA.h, evA_base[h] + qt + 3)
                        t.wait_ge(c.sPEvD.h, evD_base[h] + qt + 3)
                        t.wait_ge(c.sOEv.h, bOe + hq + qt)
                        for tcc in range(NTC):
                            ins = t.matmul(pb3[:, 0:128],
                                           vtm[:, tcc * 128:(tcc + 1) * 128],
                                           PTb[:, tcc * 128:(tcc + 1) * 128],
                                           start=(tcc == 0), stop=(tcc == NTC - 1))
                        c.sPV.tick(ins)

                @B.scalar
                def _(s):
                    rid = 0
                    if h >= 1:
                        s.wait_ge(c.sS.h, bS + 64 * hq // 16)
                    for proj in ("k", "v", "q"):
                        for ts in range(NTS):
                            s.wait_ge(c.sKp.h, kp_base[h] + 8 + rid + 1)
                            bank = pa3 if rid % 2 == 0 else pb3
                            if proj == "v" and ts >= 1:
                                s.wait_ge(c.sPT.h, pt_base[h] + ts)
                            if proj == "k":
                                ins = s.activation(kbuf[:, ts * 512:(ts + 1) * 512],
                                                   bank[:, :], AF.Copy)
                            elif proj == "q":
                                ins = s.activation(qcb[:, ts * 512:(ts + 1) * 512],
                                                   bank[:, :], AF.Copy)
                            else:
                                ins = s.activation(vstage, bank[:, :], AF.Copy)
                            c.sKpEv.tick(ins)
                            rid += 1
                            if proj == "v" and ts % 2 == 0:
                                s.wait_ge(c.sPT.h, pt_base[h] + ts + 1)
                                if h >= 1 and ts == 0:
                                    s.wait_ge(c.sPV.h, bPV + hq)
                                ins = s.activation(vtm[:, ts * 512:(ts + 1) * 512],
                                                   pT[ts % 2][:, 0:512], AF.Copy)
                                c.sPEvA.tick(ins)
                    for qt in range(16):
                        qi = hq + qt
                        s.wait_ge(c.sNm.h, bNm + 6 * (qi + 1))
                        s.wait_ge(c.sPn.h, bPn + 3 * qi)
                        if qt >= 2:
                            s.wait_ge(c.sPT.h, pt_base[h] + 4 + 2 * (qt - 1))
                        for ns in range(NTS):
                            sl = slice(ns * 512, (ns + 1) * 512)
                            ins = s.activation(Pn[qt % 2][:, sl], Spsum[:, sl],
                                               AF.Exp, bias=sfx[:, 1:2],
                                               scale=ATT_SCALE,
                                               accum_out=sfx[:, 8 + ns:9 + ns])
                            c.sExp.tick(ins)
                        s.wait_ge(c.sPT.h, pt_base[h] + 4 + 2 * qt + 1)
                        s.wait_ge(c.sPV.h, bPV + qi)
                        ins = s.activation(PTb[:, 0:1024], pT[0][:, :], AF.Copy)
                        c.sPEvA.tick(ins)

                @B.vector
                def _(v):
                    if h >= 1:
                        v.wait_ge(c.sS.h, bS + 64 * h)
                    for ts in range(NTS):
                        v.wait_ge(c.sKp.h, kp_base[h] + 2 * (ts + 1))
                        sl = slice(ts * 512, (ts + 1) * 512)
                        ins = v.tensor_mul(rsc[0:64, :], pa3[0:64, :], c.cosb[0:64, sl])
                        c.sRope.tick(ins)
                        v.wait_ge(c.sRope.h, jRo + 12 * h + 3 * ts + 1)
                        ins = v.tensor_mul(qrb[:, sl], pb3[0:64, :], c.sinb[0:64, sl])
                        c.sRope.tick(ins)
                        v.wait_ge(c.sRope.h, jRo + 12 * h + 3 * ts + 2)
                        ins = v.tensor_add(qrb[:, sl], qrb[:, sl], rsc[0:64, :])
                        c.sRope.tick(ins)
                    for ts in (1, 3):
                        v.wait_ge(c.sPT.h, pt_base[h] + ts + 1)
                        if h >= 1 and ts == 1:
                            v.wait_ge(c.sPV.h, bPV + hq)
                        ins = v.tensor_copy(vtm[:, ts * 512:(ts + 1) * 512],
                                            pT[1][:, 0:512])
                        c.sPEvD.tick(ins)
                    for qt in range(16):
                        qi = hq + qt
                        for ns in range(NTS):
                            v.wait_ge(c.sS.h, bS + 4 * qi + ns + 1)
                            ins = v.reduce_max(sfx[:, 4 + ns:5 + ns],
                                               Spsum[:, ns * 512:(ns + 1) * 512],
                                               AX.X)
                            c.sNm.tick(ins)
                            if ns == 2 and qt >= 1:
                                v.wait_ge(c.sPT.h, pt_base[h] + 4 + 2 * (qt - 1) + 2)
                                v.wait_ge(c.sPV.h, bPV + qi - 1)
                                ins = v.tensor_copy(PTb[:, 1024:2048], pT[1][:, :])
                                c.sPEvD.tick(ins)
                        v.wait_ge(c.sNm.h, bNm + 6 * qi + 4)
                        ins = v.reduce_max(sfx[:, 0:1], sfx[:, 4:8], AX.X)
                        c.sNm.tick(ins)
                        v.wait_ge(c.sNm.h, bNm + 6 * qi + 5)
                        ins = v.tensor_scalar_mul(sfx[:, 1:2], sfx[:, 0:1],
                                                  -ATT_SCALE)
                        c.sNm.tick(ins)
                        if qt >= 1:
                            v.wait_ge(c.sPV.h, bPV + qi)
                            ins = v.tensor_copy(
                                c.xno[:, h * S + (qt - 1) * 128:h * S + qt * 128],
                                pb3[:, 0:128])
                            c.sOEv.tick(ins)
                        v.wait_ge(c.sExp.h, bEx + 4 * (qi + 1))
                        ins = v.reduce_sum(sfx[:, 2:3], sfx[:, 8:12], AX.X)
                        c.sPn.tick(ins)
                        v.wait_ge(c.sPn.h, bPn + 3 * qi + 1)
                        ins = v.reciprocal(sfx[:, 3:4], sfx[:, 2:3])
                        c.sPn.tick(ins)
                        v.wait_ge(c.sPn.h, bPn + 3 * qi + 2)
                        ins = v.tensor_scalar_mul(Pn[qt % 2][:, :], Pn[qt % 2][:, :],
                                                  sfx[:, 3:4])
                        c.sPn.tick(ins)
                    v.wait_ge(c.sPT.h, pt_base[h] + 36)
                    v.wait_ge(c.sPV.h, bPV + hq + 15)
                    ins = v.tensor_copy(PTb[:, 1024:2048], pT[1][:, :])
                    c.sPEvD.tick(ins)
                    v.wait_ge(c.sPV.h, bPV + hq + 16)
                    ins = v.tensor_copy(c.xno[:, h * S + 15 * 128:h * S + 16 * 128],
                                        pb3[:, 0:128])
                    c.sOEv.tick(ins)

        assert c.sS.n == bS + H * 64
        assert c.sExp.n == bEx + 4 * H * 16
        assert c.sNm.n == bNm + 6 * H * 16
        assert c.sPn.n == bPn + 3 * H * 16
        assert c.sPV.n == bPV + H * 16 and c.sOEv.n == bOe + H * 16
        assert c.sKpEv.n == bKe + 12 * H
        assert c.sKp.n == kp_base[H]
        assert c.sPT.n == bPT + H * 36
        assert c.sPEvA.n == bEvA + 18 * H and c.sPEvD.n == bEvD + 18 * H
        assert c.sRope.n == jRo + 12 * H


def phase_P4(c, b):
    nc = c.nc
    with contextlib.ExitStack() as st:
        wob = [c.sb(st, f"wo{b}_{i}", [128, NKC * 128], BF16) for i in range(2)]
        pw4 = [c.psr(st, f"pw4{b}_{i}", [128, 512], F32) for i in range(2)]
        bP, bE = c.sWOp.n, c.sWOEv.n
        wo_tgt = [None] * NKC
        jO, jKe, jEx = c.sOEv.n, c.sKpEv.n, c.sExp.n
        jEvA, jEvD = c.sPEvA.n, c.sPEvD.n

        with nc.Block(no_gpsimd_drain=True) as B:
            @B.gpsimd
            def _(e):
                for oc in range(NKC):
                    if oc >= 2:
                        e.wait_ge(c.sWOEv.h, bE + 4 * (oc - 1))
                    ins = e.dma_start(
                        out=wob[oc % 2].rearrange("p (kc m) -> p kc m", kc=NKC),
                        in_=wcol(c.W_O, oc * 128, 128))
                    wo_tgt[oc] = c.dWO.tick(ins, oc)

            @B.tensor
            def _(t):
                t.wait_ge(c.sOEv.h, jO)
                t.wait_ge(c.sPEvA.h, jEvA)
                t.wait_ge(c.sPEvD.h, jEvD)
                t.wait_ge(c.sKpEv.h, jKe)
                t.wait_ge(c.sExp.h, jEx)
                for oc in range(NKC):
                    c.dWO.wait(t, oc, wo_tgt[oc])
                    for ts in range(NTS):
                        grp = oc * 4 + ts
                        if grp >= 2:
                            t.wait_ge(c.sWOEv.h, bE + grp - 1)
                        for kc in range(NKC):
                            ins = t.matmul(
                                pw4[grp % 2][:, :],
                                wob[oc % 2][:, kc * 128:(kc + 1) * 128],
                                c.xno[:, kc * S + ts * 512:kc * S + (ts + 1) * 512],
                                start=(kc == 0), stop=(kc == NKC - 1))
                        c.sWOp.tick(ins)

            @B.vector
            def _(v):
                for oc in range(NKC):
                    for ts in range(NTS):
                        grp = oc * 4 + ts
                        asl = c.acc_c(oc)[:, ts * 512:(ts + 1) * 512]
                        v.wait_ge(c.sWOp.h, bP + grp + 1)
                        ins = v.tensor_tensor(asl, pw4[grp % 2][:, :], asl, ALU.add)
                        c.sWOEv.tick(ins)

        assert c.sWOp.n == bP + 64 and c.sWOEv.n == bE + 64


def phase_P5(c, b):
    nc = c.nc
    with contextlib.ExitStack() as st:
        xts = [c.sb(st, f"p5x{b}_{i}", [128, DM], F32) for i in range(2)]
        ot = [c.sb(st, f"p5o{b}_{i}", [128, DM], F32) for i in range(2)]
        pt5 = [c.psr(st, f"p5p{b}_{i}", [128, S], BF16) for i in range(2)]
        bT, bEv = c.sP5T.n, c.sP5Ev.n
        jWOE = c.sWOEv.n
        x_tgt = [None] * NTC
        o_tgt = [None] * NTC

        with nc.Block(no_gpsimd_drain=True) as B:
            @B.sync
            def _(sy):
                def store(tc):
                    sy.wait_ge(c.sP5Ev.h, bEv + tc + 1)
                    ins = sy.dma_start(out=c.out[b, tc * 128:(tc + 1) * 128, :],
                                       in_=ot[tc % 2][:, :])
                    o_tgt[tc] = c.dOut.tick(ins, tc)

                for tc in range(NTC):
                    if tc >= 2:
                        sy.wait_ge(c.sP5Ev.h, bEv + tc - 1)
                    ins = sy.dma_start(out=xts[tc % 2][:, :],
                                       in_=c.x[b, tc * 128:(tc + 1) * 128, :])
                    x_tgt[tc] = c.dX.tick(ins, tc)
                    if tc >= 2:
                        store(tc - 2)
                store(NTC - 2)
                store(NTC - 1)

            @B.tensor
            def _(t):
                t.wait_ge(c.sWOEv.h, jWOE)
                for tc in range(NTC):
                    if tc >= 2:
                        t.wait_ge(c.sP5Ev.h, bEv + tc - 1)
                    for fc in range(NKC):
                        ins = t.transpose(pt5[tc % 2][:, fc * 128:(fc + 1) * 128],
                                          c.acc_c(fc)[:, tc * 128:(tc + 1) * 128],
                                          c.eyeb[:, :])
                    c.sP5T.tick(ins)

            @B.vector
            def _(v):
                for tc in range(NTC):
                    v.wait_ge(c.sP5T.h, bT + tc + 1)
                    c.dX.wait(v, tc, x_tgt[tc])
                    if tc >= 2:
                        c.dOut.wait(v, tc, o_tgt[tc - 2])
                    ins = v.tensor_tensor(ot[tc % 2][:, :], pt5[tc % 2][:, :],
                                          xts[tc % 2][:, :], ALU.add)
                    c.sP5Ev.tick(ins)

        assert c.sP5T.n == bT + NTC and c.sP5Ev.n == bEv + NTC


def make_rope_tables():
    invf = 1.0 / 10000.0 ** (np.arange(0, PD, 2, dtype=np.float64) / PD)
    t = np.arange(S, dtype=np.float64)
    ang = t[None, :] * invf[np.arange(128) % 32][:, None]
    cos = np.cos(ang).astype(np.float32)
    sin = np.sin(ang).astype(np.float32)
    sgn = np.where((np.arange(128) % 64) < 32, -1.0, 1.0).astype(np.float32)
    return cos, (sin * sgn[:, None]).astype(np.float32)


def kernel(**inputs):
    xin = np.ascontiguousarray(inputs["L0_attn_hidden_states"])
    nb = xin.shape[0] // N_CORES
    cos, sin = make_rope_tables()
    eye = np.eye(128, dtype=np.float32)
    shared = {
        "W_DKV": inputs["L0_attn_W_DKV"], "W_DQ": inputs["L0_attn_W_DQ"],
        "W_UK": inputs["L0_attn_W_UK"], "W_UV": inputs["L0_attn_W_UV"],
        "W_UQ": inputs["L0_attn_W_UQ"], "W_KR": inputs["L0_attn_W_KR"],
        "W_QR": inputs["L0_attn_W_QR"], "W_O": inputs["L0_attn_W_O"],
        "Wr": inputs["L0_ffn_Wr"], "rbias": inputs["L0_ffn_bias"],
        "Wg": inputs["L0_ffn_Wg"], "Wu": inputs["L0_ffn_Wu"],
        "Wd": inputs["L0_ffn_Wd"], "sWg": inputs["L0_ffn_sWg"],
        "sWu": inputs["L0_ffn_sWu"], "sWd": inputs["L0_ffn_sWd"],
        "eyef": eye, "ropec": cos, "ropes": sin,
    }
    shared = {k: np.ascontiguousarray(np.asarray(v, dtype=np.float32))
              for k, v in shared.items()}
    nc = build_program(nb)
    in_maps = []
    for cid in range(N_CORES):
        m = dict(shared)
        m["x"] = np.ascontiguousarray(xin[cid * nb:(cid + 1) * nb])
        in_maps.append(m)
    res = run_bass_kernel_spmd(nc, in_maps, core_ids=list(range(N_CORES)))
    return np.concatenate([np.asarray(r["out"]) for r in res.results], axis=0)


# version 05
# speedup vs baseline: 1.1282x; 1.0554x over previous
import contextlib
import math
import numpy as np
import ml_dtypes
import concourse.bass as bass
import concourse.mybir as mybir
from concourse.bass_utils import run_bass_kernel_spmd

F32 = mybir.dt.float32
BF16 = mybir.dt.bfloat16
AF = mybir.ActivationFunctionType
ALU = mybir.AluOpType
AX = mybir.AxisListType

S = 2048
DM = 2048
H, HD, LAT, PD = 16, 128, 512, 64
NE, DI = 16, 1024
NKC = DM // 128
NTC = S // 128
NTS = S // 512
NIC = DI // 128
NLC = LAT // 128
ATT_SCALE = 1.0 / math.sqrt(HD + PD)
BIG = 30000.0
N_CORES = 8
EXP = 17


class Sem:
    __slots__ = ("h", "n")

    def __init__(self, nc, name):
        self.h = nc.alloc_semaphore(name)
        self.n = 0

    def tick(self, ins, k=1):
        ins.then_inc(self.h, k)
        self.n += k
        return self.n


class DSem:

    def __init__(self, nc, name):
        self.s = [Sem(nc, name + "0"), Sem(nc, name + "1")]

    def tick(self, ins, i, k=16):
        return self.s[i % 2].tick(ins, k)

    def wait(self, eng, i, tgt):
        eng.wait_ge(self.s[i % 2].h, tgt)

    def wait_all(self, eng):
        eng.wait_ge(self.s[0].h, self.s[0].n)
        eng.wait_ge(self.s[1].h, self.s[1].n)


class Ctx:
    pass


def wcol(dram2d, c0, w):
    return dram2d[:, c0:c0 + w].rearrange("(kc p) m -> p kc m", p=128)


def build_program(nb):
    nc = bass.Bass("TRN2", target_bir_lowering=False, debug=False)
    c = Ctx()
    c.nc = nc
    c.nb = nb

    c.x = nc.dram_tensor("x", [nb, S, DM], F32, kind="ExternalInput").ap()
    for nm, sh in (("W_DKV", [DM, LAT]), ("W_DQ", [DM, LAT]), ("W_UK", [LAT, DM]),
                   ("W_UV", [LAT, DM]), ("W_UQ", [LAT, DM]), ("W_KR", [DM, PD]),
                   ("W_QR", [LAT, H * PD]), ("W_O", [DM, DM]), ("Wr", [NE, DM]),
                   ("Wg", [NE, DM, DI]), ("Wu", [NE, DM, DI]), ("Wd", [NE, DI, DM]),
                   ("sWg", [DM, DI]), ("sWu", [DM, DI]), ("sWd", [DI, DM]),
                   ("eyef", [128, 128]), ("ropec", [128, S]), ("ropes", [128, S])):
        setattr(c, nm, nc.dram_tensor(nm, sh, F32, kind="ExternalInput").ap())
    c.rbias = nc.dram_tensor("rbias", [NE], F32, kind="ExternalInput").ap()
    c.out = nc.dram_tensor("out", [nb, S, DM], F32, kind="ExternalOutput").ap()

    pes = contextlib.ExitStack()
    c.pes = pes

    def sb(stack, n, sh, d):
        return stack.enter_context(nc.sbuf_tensor(n, sh, d)).ap()

    def psr(stack, n, sh, d):
        return stack.enter_context(nc.psum_tensor(n, sh, d)).ap()

    c.sb, c.psr = sb, psr

    c.xno = sb(pes, "xno", [128, NKC * S], BF16)
    c.acc = sb(pes, "acc", [128, NKC * S], BF16)
    c.eyeb = sb(pes, "eyeb", [128, 128], BF16)
    c.eyes = sb(pes, "eyes", [128, 128], F32)
    c.cosb = sb(pes, "cosb", [128, S], BF16)
    c.sinb = sb(pes, "sinb", [128, S], BF16)
    c.wrt = sb(pes, "wrt", [128, NKC * NE], BF16)
    c.brow = sb(pes, "brow", [1, NE], BF16)
    c.ones_b = sb(pes, "ones_b", [1, 128], BF16)
    c.epsb = sb(pes, "epsb", [128, 1], F32)

    SN = ("sLd", "sStat", "sSq", "sXnT", "sXnEv",
          "sGw", "sUpG", "sUpU", "sSilu", "sH1", "sHg",
          "sDwn", "sAcc", "sGt", "sGtEv", "sRoute", "sLg", "sRtA",
          "sGb", "sGbEv", "sCp", "sCpEv", "sKp", "sRope",
          "sKw", "sKpEv", "sS", "sExp", "sNm", "sPn", "sPT", "sPTEv",
          "sPV", "sOEv", "sWOp", "sWOEv", "sP5T", "sP5Ev", "sWrtEv", "sMst", "sPEvA", "sPEvD")
    for n in SN:
        setattr(c, n, Sem(nc, n))
    c.dX = DSem(nc, "dX")
    c.dWgA = DSem(nc, "dWg")
    c.dWuA = DSem(nc, "dWu")
    c.dWd = DSem(nc, "dWd")
    c.dCw = DSem(nc, "dCw")
    c.dWO = DSem(nc, "dWO")
    c.dOut = DSem(nc, "dOut")

    c.xn_c = lambda kc: c.xno[:, kc * S:(kc + 1) * S]
    c.acc_c = lambda kc: c.acc[:, kc * S:(kc + 1) * S]

    prelude(c)
    for b in range(nb):
        phase_P0(c, b)
        phase_P1(c, b)
        with contextlib.ExitStack() as p23:
            c.cKV = sb(p23, f"cKV{b}", [128, NLC * S], BF16)
            c.cQ = sb(p23, f"cQ{b}", [128, NLC * S], BF16)
            c.kRb = sb(p23, f"kRb{b}", [64, S], BF16)
            phase_P2(c, b)
            phase_P3(c, b)
        phase_P4(c, b)
        phase_P5(c, b)

    with nc.Block(no_gpsimd_drain=True) as B:
        @B.sync
        def _(sy):
            c.dOut.wait_all(sy)

    pes.close()
    return nc


def prelude(c):
    nc = c.nc
    with contextlib.ExitStack() as st:
        ppw = [c.psr(st, f"ppw{i}", [128, 512], F32) for i in range(2)]
        wr_raw = c.sb(st, "wr_raw", [16, DM], F32)
        with nc.Block(no_gpsimd_drain=True) as B:
            @B.gpsimd
            def _(e):
                for dst, src in ((c.eyeb, c.eyef), (c.eyes, c.eyef),
                                 (c.cosb, c.ropec), (c.sinb, c.ropes)):
                    ins = e.dma_start(out=dst[:, :], in_=src[:, :])
                    c.sLd.tick(ins, 16)
                ins = e.dma_start(out=wr_raw[:, :], in_=c.Wr[:, :])
                c.sLd.tick(ins, 16)
                ins = e.dma_start(out=c.brow[0:1, :], in_=c.rbias[None, :])
                c.sLd.tick(ins, 16)
                ins = e.memset(c.ones_b[:, :], 1.0)
                c.sMst.tick(ins)
                ins = e.memset(c.epsb[:, :], 1e-5)
                c.sMst.tick(ins)

            ldn = c.sLd.n
            ev0 = c.sWrtEv.n
            gt0 = c.sGt.n

            @B.tensor
            def _(t):
                t.wait_ge(c.sLd.h, ldn)
                t.wait_ge(c.sMst.h, 2)
                for kc in range(NKC):
                    if kc >= 2:
                        t.wait_ge(c.sWrtEv.h, ev0 + kc - 1)
                    ins = t.transpose(ppw[kc % 2][:, 0:NE],
                                      wr_raw[:, kc * 128:(kc + 1) * 128],
                                      c.eyes[0:16, 0:16])
                    c.sGt.tick(ins)

            @B.vector
            def _(v):
                for kc in range(NKC):
                    v.wait_ge(c.sGt.h, gt0 + kc + 1)
                    ins = v.tensor_copy(c.wrt[:, kc * NE:(kc + 1) * NE],
                                        ppw[kc % 2][:, 0:NE])
                    c.sWrtEv.tick(ins)
        assert c.sWrtEv.n == ev0 + NKC and c.sGt.n == gt0 + NKC


def phase_P0(c, b):
    nc = c.nc
    with contextlib.ExitStack() as st:
        xts = [c.sb(st, f"p0x{b}_{i}", [128, DM], F32) for i in range(2)]
        xnt = [c.sb(st, f"p0n{b}_{i}", [128, DM], BF16) for i in range(2)]
        sq = c.sb(st, f"p0sq{b}", [128, DM], F32)
        stt = c.sb(st, f"p0st{b}", [128, 8], F32)
        ptr = [c.psr(st, f"p0p{b}_{i}", [128, S], BF16) for i in range(2)]

        bst, bsq = c.sStat.n, c.sSq.n
        bxt, bxe = c.sXnT.n, c.sXnEv.n
        bwo, bp5 = c.sWOp.n, c.sP5Ev.n
        x_tgt = [None] * NTC

        with nc.Block(no_gpsimd_drain=True) as B:
            @B.sync
            def _(sy):
                for tc in range(NTC):
                    if tc >= 2:
                        sy.wait_ge(c.sStat.h, bst + (tc - 2) * 6 + 6)
                        sy.wait_ge(c.sSq.h, bsq + (tc - 2) * 2 + 1)
                    elif b > 0:
                        c.dOut.wait_all(sy)
                    ins = sy.dma_start(out=xts[tc % 2][:, :],
                                       in_=c.x[b, tc * 128:(tc + 1) * 128, :])
                    x_tgt[tc] = c.dX.tick(ins, tc)

            @B.tensor
            def _(t):
                if b > 0:
                    t.wait_ge(c.sP5Ev.h, bp5)
                for tc in range(NTC):
                    t.wait_ge(c.sStat.h, bst + tc * 6 + 6)
                    if tc >= 2:
                        t.wait_ge(c.sXnEv.h, bxe + tc - 1)
                    for fc in range(NKC):
                        ins = t.transpose(ptr[tc % 2][:, fc * 128:(fc + 1) * 128],
                                          xnt[tc % 2][:, fc * 128:(fc + 1) * 128],
                                          c.eyeb[:, :])
                    c.sXnT.tick(ins)

            @B.scalar
            def _(s):
                if b == 0:
                    s.wait_ge(c.sMst.h, 2)
                for tc in range(NTC):
                    c.dX.wait(s, tc, x_tgt[tc])
                    if tc >= 1:
                        s.wait_ge(c.sStat.h, bst + (tc - 1) * 6 + 4)
                    ins = s.activation(sq[:, :], xts[tc % 2][:, :], AF.Square,
                                       accum_out=stt[:, 1:2])
                    c.sSq.tick(ins)
                    s.wait_ge(c.sStat.h, bst + tc * 6 + 4)
                    ins = s.activation(stt[:, 5:6], stt[:, 4:5], AF.Sqrt,
                                       bias=c.epsb[:, :])
                    c.sSq.tick(ins)

            @B.vector
            def _(v):
                for tc in range(NTC):
                    c.dX.wait(v, tc, x_tgt[tc])
                    ins = v.reduce_sum(stt[:, 0:1], xts[tc % 2][:, :], AX.X)
                    c.sStat.tick(ins)
                    v.wait_ge(c.sStat.h, bst + tc * 6 + 1)
                    ins = v.tensor_scalar_mul(stt[:, 2:3], stt[:, 0:1], 1.0 / DM)
                    c.sStat.tick(ins)
                    v.wait_ge(c.sStat.h, bst + tc * 6 + 2)
                    ins = v.scalar_tensor_tensor(stt[:, 3:4], stt[:, 2:3], 1.0,
                                                 stt[:, 2:3], ALU.mult, ALU.mult)
                    c.sStat.tick(ins)
                    v.wait_ge(c.sStat.h, bst + tc * 6 + 3)
                    v.wait_ge(c.sSq.h, bsq + tc * 2 + 1)
                    ins = v.tensor_scalar(stt[:, 4:5], stt[:, 1:2], 1.0 / DM,
                                          stt[:, 3:4], ALU.mult, ALU.subtract)
                    c.sStat.tick(ins)
                    v.wait_ge(c.sSq.h, bsq + tc * 2 + 2)
                    ins = v.reciprocal(stt[:, 6:7], stt[:, 5:6])
                    c.sStat.tick(ins)
                    v.wait_ge(c.sStat.h, bst + tc * 6 + 5)
                    if tc >= 2:
                        v.wait_ge(c.sXnT.h, bxt + tc - 1)
                    ins = v.tensor_scalar(xnt[tc % 2][:, :], xts[tc % 2][:, :],
                                          stt[:, 2:3], stt[:, 6:7],
                                          ALU.subtract, ALU.mult)
                    c.sStat.tick(ins)
                    v.wait_ge(c.sXnT.h, bxt + tc + 1)
                    if b > 0 and tc == 0:
                        v.wait_ge(c.sWOp.h, bwo)
                    dst = c.xno.rearrange("p (kc t) -> p kc t", kc=NKC)[:, :, tc * 128:(tc + 1) * 128]
                    src = ptr[tc % 2].rearrange("p (kc t) -> p kc t", kc=NKC)
                    ins = v.tensor_copy(dst, src)
                    c.sXnEv.tick(ins)

        assert c.sStat.n == bst + NTC * 6
        assert c.sSq.n == bsq + NTC * 2
        assert c.sXnT.n == bxt + NTC
        assert c.sXnEv.n == bxe + NTC


def phase_P1(c, b):
    nc = c.nc
    with contextlib.ExitStack() as st:
        wgb = [c.sb(st, f"wg{b}_{i}", [128, NKC * 128], BF16) for i in range(2)]
        wub = [c.sb(st, f"wu{b}_{i}", [128, NKC * 128], BF16) for i in range(2)]
        wdb = [c.sb(st, f"wd{b}_{i}", [128, NIC * 128], BF16) for i in range(2)]
        gateb = c.sb(st, f"gateb{b}", [128, S], BF16)
        grow = c.sb(st, f"grow{b}", [1, S], BF16)
        silsb = [c.sb(st, f"sil{b}_{i}", [128, 512], BF16) for i in range(2)]
        h1sb = [c.sb(st, f"h1{b}_{i}", [128, 512], BF16) for i in range(2)]
        hbuf = c.sb(st, f"hbuf{b}", [128, NIC * S], BF16)
        Gtm = c.sb(st, f"Gtm{b}", [128, NTC * NE], F32)
        Gfm = c.sb(st, f"Gfm{b}", [16, S], BF16)
        L = c.sb(st, f"rL{b}", [128, NE], F32)
        Lm = c.sb(st, f"rLm{b}", [128, NE], F32)
        ex = c.sb(st, f"rex{b}", [128, NE], F32)
        exm = c.sb(st, f"rexm{b}", [128, NE], F32)
        is1 = c.sb(st, f"ris1{b}", [128, NE], F32)
        is2 = c.sb(st, f"ris2{b}", [128, NE], F32)
        G2 = c.sb(st, f"rG2{b}", [128, NE], F32)
        rr = c.sb(st, f"rr{b}", [128, 12], F32)
        pg = [c.psr(st, f"pg{b}_{i}", [128, 512], F32) for i in range(2)]
        pu = [c.psr(st, f"pu{b}_{i}", [128, 512], F32) for i in range(2)]
        pd = [c.psr(st, f"pd{b}_{i}", [128, 512], F32) for i in range(2)]
        pr = [c.psr(st, f"pr{b}_{i}", [128, 512], F32) for i in range(2)]

        xn_ready = c.sXnEv.n
        wrt_ev = c.sWrtEv.n
        bp5e, bp5t = c.sP5Ev.n, c.sP5T.n
        bLg, bRt, bRA = c.sLg.n, c.sRoute.n, c.sRtA.n
        bGt, bGE = c.sGt.n, c.sGtEv.n
        bGb, bGbE, bGwd = c.sGb.n, c.sGbEv.n, c.sGw.n
        bUG, bUU = c.sUpG.n, c.sUpU.n
        wg_tgt = [None] * (EXP * NIC)
        wu_tgt = [None] * (EXP * NIC)
        wd_tgt = [None] * (EXP * NKC)
        gw_tgt = [None] * (NE + 1)
        bSi, bH1, bHg = c.sSilu.n, c.sH1.n, c.sHg.n
        bDw, bAc = c.sDwn.n, c.sAcc.n

        elist = [(c.sWg, c.sWu, c.sWd, None)] + \
                [(c.Wg[e], c.Wu[e], c.Wd[e], e) for e in range(NE)]

        with nc.Block(no_gpsimd_drain=True) as B:
            @B.gpsimd
            def _(e):
                for ep, (dg, du, dd, gi) in enumerate(elist):
                    if gi is not None:
                        e.wait_ge(c.sGtEv.h, bGE + NTC)
                        if ep >= 2:
                            e.wait_ge(c.sGb.h, bGb + 4 * (ep - 1))
                        ins = e.dma_start(out=grow[0:1, :], in_=Gfm[gi:gi + 1, :])
                        gw_tgt[ep] = c.sGw.tick(ins, 16)
                    for ic in range(NIC):
                        gu = ep * NIC + ic
                        if gu >= 2:
                            e.wait_ge(c.sUpU.h, bUU + 4 * (gu - 1))
                        ins = e.dma_start(
                            out=wgb[gu % 2].rearrange("p (kc m) -> p kc m", kc=NKC),
                            in_=wcol(dg, ic * 128, 128))
                        wg_tgt[gu] = c.dWgA.tick(ins, gu)
                        ins = e.dma_start(
                            out=wub[gu % 2].rearrange("p (kc m) -> p kc m", kc=NKC),
                            in_=wcol(du, ic * 128, 128))
                        wu_tgt[gu] = c.dWuA.tick(ins, gu)
                    for oc in range(NKC):
                        gd = ep * NKC + oc
                        if gd >= 2:
                            e.wait_ge(c.sAcc.h, bAc + 4 * (gd - 1))
                        ins = e.dma_start(
                            out=wdb[gd % 2].rearrange("p (kc m) -> p kc m", kc=NIC),
                            in_=wcol(dd, oc * 128, 128))
                        wd_tgt[gd] = c.dWd.tick(ins, gd)

            @B.tensor
            def _(t):
                t.wait_ge(c.sXnEv.h, xn_ready)
                t.wait_ge(c.sWrtEv.h, wrt_ev)
                if b > 0:
                    t.wait_ge(c.sP5Ev.h, bp5e)
                for tc in range(NTC):
                    if tc >= 2:
                        t.wait_ge(c.sRoute.h, bRt + (tc - 2) * 17 + 1)
                    for kc in range(NKC):
                        t.matmul(pr[tc % 2][:, 0:NE],
                                 c.xn_c(kc)[:, tc * 128:(tc + 1) * 128],
                                 c.wrt[:, kc * NE:(kc + 1) * NE],
                                 start=(kc == 0), stop=False)
                    ins = t.matmul(pr[tc % 2][:, 0:NE], c.ones_b[:, :],
                                   c.brow[0:1, :], start=False, stop=True)
                    c.sLg.tick(ins)
                t.wait_ge(c.sRoute.h, bRt + NTC * 17)
                for tc in range(NTC):
                    if tc >= 2:
                        t.wait_ge(c.sGtEv.h, bGE + tc - 1)
                    ins = t.transpose(pr[tc % 2][0:16, 0:128],
                                      Gtm[:, tc * NE:(tc + 1) * NE], c.eyes[:, :])
                    c.sGt.tick(ins)
                for ep, (dg, du, dd, gi) in enumerate(elist):
                    if gi is not None:
                        ri = ep - 1
                        if ep == 1:
                            t.wait_ge(c.sGtEv.h, bGE + NTC)
                        if ep >= 2:
                            t.wait_ge(c.sHg.h, bHg + 32 * ri)
                        t.wait_ge(c.sGw.h, gw_tgt[ep])
                        for q in range(4):
                            gq = 4 * ri + q
                            if gq >= 2:
                                t.wait_ge(c.sGbEv.h, bGbE + gq - 1)
                            ins = t.matmul(pr[q % 2][:, 0:512], c.ones_b[:, :],
                                           grow[0:1, q * 512:(q + 1) * 512],
                                           start=True, stop=True)
                            c.sGb.tick(ins)
                    for ic in range(NIC):
                        gu = ep * NIC + ic
                        c.dWgA.wait(t, gu, wg_tgt[gu])
                        c.dWuA.wait(t, gu, wu_tgt[gu])
                        for ts in range(NTS):
                            grp = ep * 32 + ic * 4 + ts
                            if grp >= 2:
                                t.wait_ge(c.sSilu.h, bSi + grp - 1)
                            for kc in range(NKC):
                                ins = t.matmul(pg[grp % 2][:, :],
                                               wgb[gu % 2][:, kc * 128:(kc + 1) * 128],
                                               c.xn_c(kc)[:, ts * 512:(ts + 1) * 512],
                                               start=(kc == 0), stop=(kc == NKC - 1))
                            c.sUpG.tick(ins)
                            if grp >= 2:
                                t.wait_ge(c.sH1.h, bH1 + grp - 1)
                            for kc in range(NKC):
                                ins = t.matmul(pu[grp % 2][:, :],
                                               wub[gu % 2][:, kc * 128:(kc + 1) * 128],
                                               c.xn_c(kc)[:, ts * 512:(ts + 1) * 512],
                                               start=(kc == 0), stop=(kc == NKC - 1))
                            c.sUpU.tick(ins)
                    for oc in range(NKC):
                        gd = ep * NKC + oc
                        c.dWd.wait(t, gd, wd_tgt[gd])
                        for ts in range(NTS):
                            dgrp = ep * 64 + oc * 4 + ts
                            if dgrp >= 2:
                                t.wait_ge(c.sAcc.h, bAc + dgrp - 1)
                            if oc == 0:
                                if gi is None:
                                    t.wait_ge(c.sH1.h, bH1 + ep * 32 + 29 + ts)
                                else:
                                    t.wait_ge(c.sHg.h, bHg + 32 * (ep - 1) + 29 + ts)
                            for kc in range(NIC):
                                ins = t.matmul(
                                    pd[dgrp % 2][:, :],
                                    wdb[gd % 2][:, kc * 128:(kc + 1) * 128],
                                    hbuf[:, kc * S + ts * 512:kc * S + (ts + 1) * 512],
                                    start=(kc == 0), stop=(kc == NIC - 1))
                            c.sDwn.tick(ins)

            @B.scalar
            def _(s):
                def emit_silu(grp):
                    s.wait_ge(c.sUpG.h, bUG + grp + 1)
                    if grp >= 2:
                        s.wait_ge(c.sH1.h, bH1 + grp - 1)
                    ins = s.activation(silsb[grp % 2][:, :], pg[grp % 2][:, :],
                                       AF.Silu)
                    c.sSilu.tick(ins)

                for tc in range(NTC):
                    s.wait_ge(c.sRoute.h, bRt + tc * 17 + 4)
                    ins = s.activation(ex[:, :], L[:, :], AF.Exp,
                                       bias=rr[:, 1:2], accum_out=rr[:, 2:3])
                    c.sRtA.tick(ins)
                for sg in range(EXP * 32):
                    emit_silu(sg)

            @B.vector
            def _(v):
                for tc in range(NTC):
                    base = bRt + tc * 17
                    v.wait_ge(c.sLg.h, bLg + tc + 1)
                    if tc >= 1:
                        v.wait_ge(c.sRtA.h, bRA + tc)
                    ins = v.tensor_copy(L[:, :], pr[tc % 2][:, 0:NE])
                    c.sRoute.tick(ins)
                    steps = [
                        lambda: v.reduce_max(rr[:, 0:1], L[:, :], AX.X),
                        lambda: v.tensor_single_scalar(is1[:, :], L[:, :], rr[:, 0:1], ALU.is_ge),
                        lambda: v.tensor_scalar_mul(rr[:, 1:2], rr[:, 0:1], -1.0),
                        lambda: v.scalar_tensor_tensor(Lm[:, :], is1[:, :], -BIG, L[:, :], ALU.mult, ALU.add),
                        lambda: v.reduce_max(rr[:, 3:4], Lm[:, :], AX.X),
                        lambda: v.tensor_single_scalar(is2[:, :], Lm[:, :], rr[:, 3:4], ALU.is_ge),
                        None,
                        lambda: v.scalar_tensor_tensor(exm[:, :], is1[:, :], -4.0, ex[:, :], ALU.mult, ALU.add),
                        lambda: v.reduce_max(rr[:, 5:6], exm[:, :], AX.X),
                        lambda: v.tensor_add(rr[:, 6:7], rr[:, 4:5], rr[:, 5:6]),
                        lambda: v.scalar_tensor_tensor(rr[:, 7:8], rr[:, 2:3], 1e-9, rr[:, 6:7], ALU.mult, ALU.add),
                        lambda: v.reciprocal(rr[:, 8:9], rr[:, 7:8]),
                        lambda: v.tensor_mul(rr[:, 9:10], rr[:, 4:5], rr[:, 8:9]),
                        lambda: v.tensor_mul(rr[:, 10:11], rr[:, 5:6], rr[:, 8:9]),
                        lambda: v.tensor_scalar_mul(G2[:, :], is1[:, :], rr[:, 9:10]),
                        lambda: v.scalar_tensor_tensor(
                            Gtm[:, tc * NE:(tc + 1) * NE], is2[:, :], rr[:, 10:11],
                            G2[:, :], ALU.mult, ALU.add),
                    ]
                    for i, f in enumerate(steps):
                        v.wait_ge(c.sRoute.h, base + i + 1)
                        if f is None:
                            v.wait_ge(c.sRtA.h, bRA + tc + 1)
                            ins = v.reduce_max(rr[:, 4:5], ex[:, :], AX.X)
                        else:
                            ins = f()
                        c.sRoute.tick(ins)
                for tc in range(NTC):
                    v.wait_ge(c.sGt.h, bGt + tc + 1)
                    ins = v.tensor_copy(Gfm[0:16, tc * 128:(tc + 1) * 128],
                                        pr[tc % 2][0:16, 0:128])
                    c.sGtEv.tick(ins)
                for ep, (dg, du, dd, gi) in enumerate(elist):
                    if gi is not None:
                        for q in range(4):
                            gq = 4 * (ep - 1) + q
                            v.wait_ge(c.sGb.h, bGb + gq + 1)
                            ins = v.tensor_copy(gateb[:, q * 512:(q + 1) * 512],
                                                pr[q % 2][:, 0:512])
                            c.sGbEv.tick(ins)
                    for ic in range(NIC):
                        for ts in range(NTS):
                            grp = ep * 32 + ic * 4 + ts
                            hs = hbuf[:, ic * S + ts * 512:ic * S + (ts + 1) * 512]
                            v.wait_ge(c.sSilu.h, bSi + grp + 1)
                            v.wait_ge(c.sUpU.h, bUU + grp + 1)
                            if gi is None:
                                ins = v.tensor_mul(hs, silsb[grp % 2][:, :],
                                                   pu[grp % 2][:, :])
                                c.sH1.tick(ins)
                            else:
                                ins = v.tensor_mul(h1sb[grp % 2][:, :],
                                                   silsb[grp % 2][:, :],
                                                   pu[grp % 2][:, :])
                                c.sH1.tick(ins)
                                v.wait_ge(c.sH1.h, bH1 + grp + 1)
                                ins = v.tensor_mul(hs, h1sb[grp % 2][:, :],
                                                   gateb[:, ts * 512:(ts + 1) * 512])
                                c.sHg.tick(ins)
                    for oc in range(NKC):
                        for ts in range(NTS):
                            dgrp = ep * 64 + oc * 4 + ts
                            asl = c.acc_c(oc)[:, ts * 512:(ts + 1) * 512]
                            v.wait_ge(c.sDwn.h, bDw + dgrp + 1)
                            if ep == 0:
                                if b > 0 and dgrp == 0:
                                    v.wait_ge(c.sP5T.h, bp5t)
                                ins = v.tensor_copy(asl, pd[dgrp % 2][:, :])
                            else:
                                ins = v.tensor_tensor(asl, pd[dgrp % 2][:, :],
                                                      asl, ALU.add)
                            c.sAcc.tick(ins)

        assert c.sLg.n == bLg + NTC
        assert c.sRoute.n == bRt + NTC * 17
        assert c.sRtA.n == bRA + NTC
        assert c.sGt.n == bGt + NTC and c.sGtEv.n == bGE + NTC
        assert c.sGb.n == bGb + 64 and c.sGbEv.n == bGbE + 64
        assert c.sGw.n == bGwd + 16 * NE
        assert c.sUpG.n == bUG + EXP * 32 and c.sUpU.n == bUU + EXP * 32
        assert c.sSilu.n == bSi + EXP * 32
        assert c.sH1.n == bH1 + EXP * 32
        assert c.sHg.n == bHg + NE * 32
        assert c.sDwn.n == bDw + EXP * 64 and c.sAcc.n == bAc + EXP * 64


def phase_P2(c, b):
    nc = c.nc
    with contextlib.ExitStack() as st:
        wcb = [c.sb(st, f"wc{b}_{i}", [128, NKC * 128], BF16) for i in range(2)]
        krs = c.sb(st, f"krs{b}", [64, 512], F32)
        pab = [c.psr(st, f"pc{b}_{i}", [128, 512], F32) for i in range(2)]
        pn = c.psr(st, f"pn{b}", [128, 512], F32)
        ps2 = c.psr(st, f"ps2{b}", [128, 512], F32)

        bCp, bCe = c.sCp.n, c.sCpEv.n
        cw_tgt = [None] * 10
        bKp, bRo = c.sKp.n, c.sRope.n
        jAcc, jSil, jH1 = c.sAcc.n, c.sSilu.n, c.sH1.n
        jGbE, jGtE = c.sGbEv.n, c.sGtEv.n

        chunks = [wcol(c.W_DKV, mc * 128, 128) for mc in range(NLC)] + \
                 [wcol(c.W_DQ, mc * 128, 128) for mc in range(NLC)]

        with nc.Block(no_gpsimd_drain=True) as B:
            @B.gpsimd
            def _(e):
                for j, src in enumerate(chunks):
                    if j >= 2:
                        e.wait_ge(c.sCp.h, bCp + 4 * (j - 1))
                    ins = e.dma_start(
                        out=wcb[j % 2].rearrange("p (kc m) -> p kc m", kc=NKC),
                        in_=src)
                    cw_tgt[j] = c.dCw.tick(ins, j)
                e.wait_ge(c.sCp.h, bCp + 4 * 7)
                ins = e.dma_start(
                    out=wcb[0].rearrange("p (kc m) -> p kc m", kc=NKC)[:, :, 0:64],
                    in_=wcol(c.W_KR, 0, 64))
                cw_tgt[8] = c.dCw.tick(ins, 8)
                e.wait_ge(c.sCp.h, bCp + 4 * 8)
                ins = e.dma_start(
                    out=wcb[1].rearrange("p (kc m) -> p kc m", kc=NKC)[:, :, 0:32],
                    in_=wcol(c.W_KR, 32, 32))
                c.dCw.tick(ins, 9)
                ins = e.dma_start(
                    out=wcb[1].rearrange("p (kc m) -> p kc m", kc=NKC)[:, :, 32:64],
                    in_=wcol(c.W_KR, 0, 32))
                cw_tgt[9] = c.dCw.tick(ins, 9)

            @B.tensor
            def _(t):
                t.wait_ge(c.sAcc.h, jAcc)
                t.wait_ge(c.sSilu.h, jSil)
                t.wait_ge(c.sH1.h, jH1)
                t.wait_ge(c.sGbEv.h, jGbE)
                t.wait_ge(c.sGtEv.h, jGtE)
                for j in range(8):
                    c.dCw.wait(t, j, cw_tgt[j])
                    for ts in range(NTS):
                        r = j * 4 + ts
                        if r >= 2:
                            t.wait_ge(c.sCpEv.h, bCe + r - 1)
                        for kc in range(NKC):
                            ins = t.matmul(pab[r % 2][:, :],
                                           wcb[j % 2][:, kc * 128:(kc + 1) * 128],
                                           c.xn_c(kc)[:, ts * 512:(ts + 1) * 512],
                                           start=(kc == 0), stop=(kc == NKC - 1))
                        c.sCp.tick(ins)
                c.dCw.wait(t, 8, cw_tgt[8])
                c.dCw.wait(t, 9, cw_tgt[9])
                for ts in range(NTS):
                    if ts >= 1:
                        t.wait_ge(c.sRope.h, bRo + 3 * ts)
                    for kc in range(NKC):
                        ins = t.matmul(pn[0:64, :],
                                       wcb[0][:, kc * 128:kc * 128 + 64],
                                       c.xn_c(kc)[:, ts * 512:(ts + 1) * 512],
                                       start=(kc == 0), stop=(kc == NKC - 1))
                    c.sKp.tick(ins)
                    for kc in range(NKC):
                        ins = t.matmul(ps2[0:64, :],
                                       wcb[1][:, kc * 128:kc * 128 + 64],
                                       c.xn_c(kc)[:, ts * 512:(ts + 1) * 512],
                                       start=(kc == 0), stop=(kc == NKC - 1))
                    c.sKp.tick(ins)

            @B.vector
            def _(v):
                for j in range(8):
                    dst = c.cKV if j < NLC else c.cQ
                    mc = j % NLC
                    for ts in range(NTS):
                        r = j * 4 + ts
                        v.wait_ge(c.sCp.h, bCp + r + 1)
                        ins = v.tensor_copy(
                            dst[:, mc * S + ts * 512:mc * S + (ts + 1) * 512],
                            pab[r % 2][:, :])
                        c.sCpEv.tick(ins)
                for ts in range(NTS):
                    v.wait_ge(c.sKp.h, bKp + 2 * (ts + 1))
                    sl = slice(ts * 512, (ts + 1) * 512)
                    ins = v.tensor_mul(krs[:, :], pn[0:64, :], c.cosb[0:64, sl])
                    c.sRope.tick(ins)
                    v.wait_ge(c.sRope.h, bRo + 3 * ts + 1)
                    ins = v.tensor_mul(c.kRb[:, sl], ps2[0:64, :], c.sinb[0:64, sl])
                    c.sRope.tick(ins)
                    v.wait_ge(c.sRope.h, bRo + 3 * ts + 2)
                    ins = v.tensor_add(c.kRb[:, sl], c.kRb[:, sl], krs[:, :])
                    c.sRope.tick(ins)

        assert c.sCp.n == bCp + 32 and c.sCpEv.n == bCe + 32
        assert c.sKp.n == bKp + 8
        assert c.sRope.n == bRo + 12


def phase_P3(c, b):
    nc = c.nc
    with contextlib.ExitStack() as st:
        kbuf = c.sb(st, f"kbuf{b}", [128, S], BF16)
        qcb = c.sb(st, f"qcb{b}", [128, S], BF16)
        qrb = c.sb(st, f"qrb{b}", [64, S], BF16)
        vtm = c.sb(st, f"vtm{b}", [128, NTC * 128], BF16)
        rsc = c.sb(st, f"rsc{b}", [128, 512], F32)
        ukb = c.sb(st, f"uk{b}", [128, 512], BF16)
        uvb = c.sb(st, f"uv{b}", [128, 512], BF16)
        uqb = c.sb(st, f"uq{b}", [128, 512], BF16)
        vstage = qcb[:, 0:512]
        qrn = c.sb(st, f"qn{b}", [128, NLC * 64], BF16)
        qrs = c.sb(st, f"qs{b}", [128, NLC * 64], BF16)
        Pn = [c.sb(st, f"Pn{b}_{i}", [128, S], BF16) for i in range(2)]
        PTb = c.sb(st, f"PTb{b}", [128, NTC * 128], BF16)
        sfx = c.sb(st, f"sfx{b}", [128, 16], F32)
        Spsum = c.psr(st, f"Spsum{b}", [128, S], F32)
        pa3 = c.psr(st, f"pa3{b}", [128, 512], F32)
        pb3 = c.psr(st, f"pb3{b}", [128, 512], F32)
        pT = [c.psr(st, f"pT{b}_{i}", [128, 1024], BF16) for i in range(2)]

        bKw, bKp, bKe = c.sKw.n, c.sKp.n, c.sKpEv.n
        bS, bEx, bNm, bPn = c.sS.n, c.sExp.n, c.sNm.n, c.sPn.n
        bPT, bPV, bOe = c.sPT.n, c.sPV.n, c.sOEv.n
        bEvA, bEvD = c.sPEvA.n, c.sPEvD.n
        jCpE, jRo = c.sCpEv.n, c.sRope.n

        kp_base = [bKp + 20 * h for h in range(H + 1)]
        pt_base = [bPT + 36 * h for h in range(H + 1)]
        evA_base = [bEvA + 18 * h for h in range(H + 1)]
        evD_base = [bEvD + 18 * h for h in range(H + 1)]

        def head_dmas(e, h):
            for wb, src in ((ukb, c.W_UK), (uvb, c.W_UV), (uqb, c.W_UQ)):
                ins = e.dma_start(
                    out=wb.rearrange("p (kc m) -> p kc m", kc=NLC),
                    in_=wcol(src, h * 128, 128))
                c.sKw.tick(ins, 16)
            ins = e.dma_start(
                out=qrn.rearrange("p (kc m) -> p kc m", kc=NLC),
                in_=wcol(c.W_QR, h * 64, 64))
            c.sKw.tick(ins, 16)
            dst = qrs.rearrange("p (kc m) -> p kc m", kc=NLC)
            ins = e.dma_start(out=dst[:, :, 0:32],
                              in_=wcol(c.W_QR, h * 64 + 32, 32))
            c.sKw.tick(ins, 16)
            ins = e.dma_start(out=dst[:, :, 32:64],
                              in_=wcol(c.W_QR, h * 64, 32))
            c.sKw.tick(ins, 16)

        dma_cum = [6 * (j + 1) for j in range(H)]

        for h in range(H):
            hq = h * 16
            with nc.Block(no_gpsimd_drain=True) as B:
                @B.gpsimd
                def _(e):
                    if h == 0:
                        head_dmas(e, 0)
                    if h + 1 < H:
                        e.wait_ge(c.sKp.h, kp_base[h + 1])
                        head_dmas(e, h + 1)

                @B.tensor
                def _(t):
                    if h == 0:
                        t.wait_ge(c.sCpEv.h, jCpE)
                        t.wait_ge(c.sRope.h, jRo)
                    t.wait_ge(c.sKw.h, bKw + 16 * dma_cum[h])
                    t.wait_ge(c.sOEv.h, bOe + hq)
                    t.wait_ge(c.sKpEv.h, bKe + 12 * h)
                    for ts in range(NTS):
                        if ts >= 1:
                            t.wait_ge(c.sRope.h, jRo + 12 * h + 3 * ts)
                        for kc in range(NLC):
                            ins = t.matmul(
                                pa3[0:64, :],
                                qrn[:, kc * 64:(kc + 1) * 64],
                                c.cQ[:, kc * S + ts * 512:kc * S + (ts + 1) * 512],
                                start=(kc == 0), stop=(kc == NLC - 1))
                        c.sKp.tick(ins)
                        for kc in range(NLC):
                            ins = t.matmul(
                                pb3[0:64, :],
                                qrs[:, kc * 64:(kc + 1) * 64],
                                c.cQ[:, kc * S + ts * 512:kc * S + (ts + 1) * 512],
                                start=(kc == 0), stop=(kc == NLC - 1))
                        c.sKp.tick(ins)
                    rid = 0
                    for proj, wb, csrc in (("k", ukb, c.cKV), ("v", uvb, c.cKV),
                                           ("q", uqb, c.cQ)):
                        for ts in range(NTS):
                            if rid >= 2:
                                t.wait_ge(c.sKpEv.h, bKe + 12 * h + rid - 1)
                            elif rid == 0:
                                t.wait_ge(c.sRope.h, jRo + 12 * (h + 1))
                            bank = pa3 if rid % 2 == 0 else pb3
                            for kc in range(NLC):
                                ins = t.matmul(
                                    bank[:, :],
                                    wb[:, kc * 128:(kc + 1) * 128],
                                    csrc[:, kc * S + ts * 512:kc * S + (ts + 1) * 512],
                                    start=(kc == 0), stop=(kc == NLC - 1))
                            c.sKp.tick(ins)
                            rid += 1
                            if proj == "v":
                                t.wait_ge(c.sKpEv.h, bKe + 12 * h + rid)
                                g = ts
                                sem, eb = ((c.sPEvA, evA_base) if g % 2 == 0
                                           else (c.sPEvD, evD_base))
                                t.wait_ge(sem.h, eb[h] + g // 2)
                                for j in range(4):
                                    ins = t.transpose(pT[g % 2][:, j * 128:(j + 1) * 128],
                                                      vstage[:, j * 128:(j + 1) * 128],
                                                      c.eyeb[:, :])
                                c.sPT.tick(ins)

                    def S_mm(qt):
                        for ns in range(NTS):
                            if qt > 0:
                                t.wait_ge(c.sExp.h, bEx + 4 * (hq + qt - 1) + ns + 1)
                            t.matmul(Spsum[:, ns * 512:(ns + 1) * 512],
                                     qcb[:, qt * 128:(qt + 1) * 128],
                                     kbuf[:, ns * 512:(ns + 1) * 512],
                                     start=True, stop=False)
                        for ns in range(NTS):
                            ins = t.matmul(Spsum[:, ns * 512:(ns + 1) * 512],
                                           qrb[:, qt * 128:(qt + 1) * 128],
                                           c.kRb[:, ns * 512:(ns + 1) * 512],
                                           start=False, stop=True)
                            c.sS.tick(ins)

                    t.wait_ge(c.sKpEv.h, bKe + 12 * (h + 1))
                    t.wait_ge(c.sRope.h, jRo + 12 * (h + 1))
                    t.wait_ge(c.sExp.h, bEx + 4 * hq)
                    S_mm(0)
                    for qt in range(16):
                        if qt < 15:
                            S_mm(qt + 1)
                        for r2 in range(2):
                            if r2 == 0:
                                t.wait_ge(c.sPn.h, bPn + 3 * (hq + qt + 1))
                            g = 4 + 2 * qt + r2
                            sem, eb = ((c.sPEvA, evA_base) if g % 2 == 0
                                       else (c.sPEvD, evD_base))
                            t.wait_ge(sem.h, eb[h] + g // 2)
                            for j in range(8):
                                tcc = r2 * 8 + j
                                ins = t.transpose(
                                    pT[g % 2][:, j * 128:(j + 1) * 128],
                                    Pn[qt % 2][:, tcc * 128:(tcc + 1) * 128],
                                    c.eyeb[:, :])
                            c.sPT.tick(ins)
                        t.wait_ge(c.sPEvA.h, evA_base[h] + qt + 3)
                        t.wait_ge(c.sPEvD.h, evD_base[h] + qt + 3)
                        t.wait_ge(c.sOEv.h, bOe + hq + qt)
                        for tcc in range(NTC):
                            ins = t.matmul(pb3[:, 0:128],
                                           vtm[:, tcc * 128:(tcc + 1) * 128],
                                           PTb[:, tcc * 128:(tcc + 1) * 128],
                                           start=(tcc == 0), stop=(tcc == NTC - 1))
                        c.sPV.tick(ins)

                @B.scalar
                def _(s):
                    rid = 0
                    if h >= 1:
                        s.wait_ge(c.sS.h, bS + 64 * hq // 16)
                    for proj in ("k", "v", "q"):
                        for ts in range(NTS):
                            s.wait_ge(c.sKp.h, kp_base[h] + 8 + rid + 1)
                            bank = pa3 if rid % 2 == 0 else pb3
                            if proj == "v" and ts >= 1:
                                s.wait_ge(c.sPT.h, pt_base[h] + ts)
                            if proj == "k":
                                ins = s.activation(kbuf[:, ts * 512:(ts + 1) * 512],
                                                   bank[:, :], AF.Copy)
                            elif proj == "q":
                                ins = s.activation(qcb[:, ts * 512:(ts + 1) * 512],
                                                   bank[:, :], AF.Copy)
                            else:
                                ins = s.activation(vstage, bank[:, :], AF.Copy)
                            c.sKpEv.tick(ins)
                            rid += 1
                            if proj == "v" and ts % 2 == 0:
                                s.wait_ge(c.sPT.h, pt_base[h] + ts + 1)
                                if h >= 1 and ts == 0:
                                    s.wait_ge(c.sPV.h, bPV + hq)
                                ins = s.activation(vtm[:, ts * 512:(ts + 1) * 512],
                                                   pT[ts % 2][:, 0:512], AF.Copy)
                                c.sPEvA.tick(ins)
                    for qt in range(16):
                        qi = hq + qt
                        s.wait_ge(c.sNm.h, bNm + 6 * (qi + 1))
                        s.wait_ge(c.sPn.h, bPn + 3 * qi)
                        if qt >= 2:
                            s.wait_ge(c.sPT.h, pt_base[h] + 4 + 2 * (qt - 1))
                        for ns in range(NTS):
                            sl = slice(ns * 512, (ns + 1) * 512)
                            ins = s.activation(Pn[qt % 2][:, sl], Spsum[:, sl],
                                               AF.Exp, bias=sfx[:, 1:2],
                                               scale=ATT_SCALE,
                                               accum_out=sfx[:, 8 + ns:9 + ns])
                            c.sExp.tick(ins)
                        s.wait_ge(c.sPT.h, pt_base[h] + 4 + 2 * qt + 1)
                        s.wait_ge(c.sPV.h, bPV + qi)
                        ins = s.activation(PTb[:, 0:1024], pT[0][:, :], AF.Copy)
                        c.sPEvA.tick(ins)

                @B.vector
                def _(v):
                    if h >= 1:
                        v.wait_ge(c.sS.h, bS + 64 * h)
                    for ts in range(NTS):
                        v.wait_ge(c.sKp.h, kp_base[h] + 2 * (ts + 1))
                        sl = slice(ts * 512, (ts + 1) * 512)
                        ins = v.tensor_mul(rsc[0:64, :], pa3[0:64, :], c.cosb[0:64, sl])
                        c.sRope.tick(ins)
                        v.wait_ge(c.sRope.h, jRo + 12 * h + 3 * ts + 1)
                        ins = v.tensor_mul(qrb[:, sl], pb3[0:64, :], c.sinb[0:64, sl])
                        c.sRope.tick(ins)
                        v.wait_ge(c.sRope.h, jRo + 12 * h + 3 * ts + 2)
                        ins = v.tensor_add(qrb[:, sl], qrb[:, sl], rsc[0:64, :])
                        c.sRope.tick(ins)
                    for ts in (1, 3):
                        v.wait_ge(c.sPT.h, pt_base[h] + ts + 1)
                        if h >= 1 and ts == 1:
                            v.wait_ge(c.sPV.h, bPV + hq)
                        ins = v.tensor_copy(vtm[:, ts * 512:(ts + 1) * 512],
                                            pT[1][:, 0:512])
                        c.sPEvD.tick(ins)
                    for qt in range(16):
                        qi = hq + qt
                        for ns in range(NTS):
                            v.wait_ge(c.sS.h, bS + 4 * qi + ns + 1)
                            ins = v.reduce_max(sfx[:, 4 + ns:5 + ns],
                                               Spsum[:, ns * 512:(ns + 1) * 512],
                                               AX.X)
                            c.sNm.tick(ins)
                        v.wait_ge(c.sNm.h, bNm + 6 * qi + 4)
                        ins = v.reduce_max(sfx[:, 0:1], sfx[:, 4:8], AX.X)
                        c.sNm.tick(ins)
                        v.wait_ge(c.sNm.h, bNm + 6 * qi + 5)
                        ins = v.tensor_scalar_mul(sfx[:, 1:2], sfx[:, 0:1],
                                                  -ATT_SCALE)
                        c.sNm.tick(ins)
                        if qt >= 1:
                            v.wait_ge(c.sPT.h, pt_base[h] + 4 + 2 * (qt - 1) + 2)
                            v.wait_ge(c.sPV.h, bPV + qi - 1)
                            ins = v.tensor_copy(PTb[:, 1024:2048], pT[1][:, :])
                            c.sPEvD.tick(ins)
                            v.wait_ge(c.sPV.h, bPV + qi)
                            ins = v.tensor_copy(
                                c.xno[:, h * S + (qt - 1) * 128:h * S + qt * 128],
                                pb3[:, 0:128])
                            c.sOEv.tick(ins)
                        v.wait_ge(c.sExp.h, bEx + 4 * (qi + 1))
                        ins = v.reduce_sum(sfx[:, 2:3], sfx[:, 8:12], AX.X)
                        c.sPn.tick(ins)
                        v.wait_ge(c.sPn.h, bPn + 3 * qi + 1)
                        ins = v.reciprocal(sfx[:, 3:4], sfx[:, 2:3])
                        c.sPn.tick(ins)
                        v.wait_ge(c.sPn.h, bPn + 3 * qi + 2)
                        ins = v.tensor_scalar_mul(Pn[qt % 2][:, :], Pn[qt % 2][:, :],
                                                  sfx[:, 3:4])
                        c.sPn.tick(ins)
                    v.wait_ge(c.sPT.h, pt_base[h] + 36)
                    v.wait_ge(c.sPV.h, bPV + hq + 15)
                    ins = v.tensor_copy(PTb[:, 1024:2048], pT[1][:, :])
                    c.sPEvD.tick(ins)
                    v.wait_ge(c.sPV.h, bPV + hq + 16)
                    ins = v.tensor_copy(c.xno[:, h * S + 15 * 128:h * S + 16 * 128],
                                        pb3[:, 0:128])
                    c.sOEv.tick(ins)

        assert c.sS.n == bS + H * 64
        assert c.sExp.n == bEx + 4 * H * 16
        assert c.sNm.n == bNm + 6 * H * 16
        assert c.sPn.n == bPn + 3 * H * 16
        assert c.sPV.n == bPV + H * 16 and c.sOEv.n == bOe + H * 16
        assert c.sKpEv.n == bKe + 12 * H
        assert c.sKp.n == kp_base[H]
        assert c.sPT.n == bPT + H * 36
        assert c.sPEvA.n == bEvA + 18 * H and c.sPEvD.n == bEvD + 18 * H
        assert c.sRope.n == jRo + 12 * H


def phase_P4(c, b):
    nc = c.nc
    with contextlib.ExitStack() as st:
        wob = [c.sb(st, f"wo{b}_{i}", [128, NKC * 128], BF16) for i in range(2)]
        pw4 = [c.psr(st, f"pw4{b}_{i}", [128, 512], F32) for i in range(2)]
        bP, bE = c.sWOp.n, c.sWOEv.n
        wo_tgt = [None] * NKC
        jO, jKe, jEx = c.sOEv.n, c.sKpEv.n, c.sExp.n
        jEvA, jEvD = c.sPEvA.n, c.sPEvD.n

        with nc.Block(no_gpsimd_drain=True) as B:
            @B.gpsimd
            def _(e):
                for oc in range(NKC):
                    if oc >= 2:
                        e.wait_ge(c.sWOEv.h, bE + 4 * (oc - 1))
                    ins = e.dma_start(
                        out=wob[oc % 2].rearrange("p (kc m) -> p kc m", kc=NKC),
                        in_=wcol(c.W_O, oc * 128, 128))
                    wo_tgt[oc] = c.dWO.tick(ins, oc)

            @B.tensor
            def _(t):
                t.wait_ge(c.sOEv.h, jO)
                t.wait_ge(c.sPEvA.h, jEvA)
                t.wait_ge(c.sPEvD.h, jEvD)
                t.wait_ge(c.sKpEv.h, jKe)
                t.wait_ge(c.sExp.h, jEx)
                for oc in range(NKC):
                    c.dWO.wait(t, oc, wo_tgt[oc])
                    for ts in range(NTS):
                        grp = oc * 4 + ts
                        if grp >= 2:
                            t.wait_ge(c.sWOEv.h, bE + grp - 1)
                        for kc in range(NKC):
                            ins = t.matmul(
                                pw4[grp % 2][:, :],
                                wob[oc % 2][:, kc * 128:(kc + 1) * 128],
                                c.xno[:, kc * S + ts * 512:kc * S + (ts + 1) * 512],
                                start=(kc == 0), stop=(kc == NKC - 1))
                        c.sWOp.tick(ins)

            @B.vector
            def _(v):
                for oc in range(NKC):
                    for ts in range(NTS):
                        grp = oc * 4 + ts
                        asl = c.acc_c(oc)[:, ts * 512:(ts + 1) * 512]
                        v.wait_ge(c.sWOp.h, bP + grp + 1)
                        ins = v.tensor_tensor(asl, pw4[grp % 2][:, :], asl, ALU.add)
                        c.sWOEv.tick(ins)

        assert c.sWOp.n == bP + 64 and c.sWOEv.n == bE + 64


def phase_P5(c, b):
    nc = c.nc
    with contextlib.ExitStack() as st:
        xts = [c.sb(st, f"p5x{b}_{i}", [128, DM], F32) for i in range(2)]
        ot = [c.sb(st, f"p5o{b}_{i}", [128, DM], F32) for i in range(2)]
        pt5 = [c.psr(st, f"p5p{b}_{i}", [128, S], BF16) for i in range(2)]
        bT, bEv = c.sP5T.n, c.sP5Ev.n
        jWOE = c.sWOEv.n
        x_tgt = [None] * NTC
        o_tgt = [None] * NTC

        with nc.Block(no_gpsimd_drain=True) as B:
            @B.sync
            def _(sy):
                def store(tc):
                    sy.wait_ge(c.sP5Ev.h, bEv + tc + 1)
                    ins = sy.dma_start(out=c.out[b, tc * 128:(tc + 1) * 128, :],
                                       in_=ot[tc % 2][:, :])
                    o_tgt[tc] = c.dOut.tick(ins, tc)

                for tc in range(NTC):
                    if tc >= 2:
                        sy.wait_ge(c.sP5Ev.h, bEv + tc - 1)
                    ins = sy.dma_start(out=xts[tc % 2][:, :],
                                       in_=c.x[b, tc * 128:(tc + 1) * 128, :])
                    x_tgt[tc] = c.dX.tick(ins, tc)
                    if tc >= 2:
                        store(tc - 2)
                store(NTC - 2)
                store(NTC - 1)

            @B.tensor
            def _(t):
                t.wait_ge(c.sWOEv.h, jWOE)
                for tc in range(NTC):
                    if tc >= 2:
                        t.wait_ge(c.sP5Ev.h, bEv + tc - 1)
                    for fc in range(NKC):
                        ins = t.transpose(pt5[tc % 2][:, fc * 128:(fc + 1) * 128],
                                          c.acc_c(fc)[:, tc * 128:(tc + 1) * 128],
                                          c.eyeb[:, :])
                    c.sP5T.tick(ins)

            @B.vector
            def _(v):
                for tc in range(NTC):
                    v.wait_ge(c.sP5T.h, bT + tc + 1)
                    c.dX.wait(v, tc, x_tgt[tc])
                    if tc >= 2:
                        c.dOut.wait(v, tc, o_tgt[tc - 2])
                    ins = v.tensor_tensor(ot[tc % 2][:, :], pt5[tc % 2][:, :],
                                          xts[tc % 2][:, :], ALU.add)
                    c.sP5Ev.tick(ins)

        assert c.sP5T.n == bT + NTC and c.sP5Ev.n == bEv + NTC


def make_rope_tables():
    invf = 1.0 / 10000.0 ** (np.arange(0, PD, 2, dtype=np.float64) / PD)
    t = np.arange(S, dtype=np.float64)
    ang = t[None, :] * invf[np.arange(128) % 32][:, None]
    cos = np.cos(ang).astype(np.float32)
    sin = np.sin(ang).astype(np.float32)
    sgn = np.where((np.arange(128) % 64) < 32, -1.0, 1.0).astype(np.float32)
    return cos, (sin * sgn[:, None]).astype(np.float32)


def kernel(**inputs):
    xin = np.ascontiguousarray(inputs["L0_attn_hidden_states"])
    nb = xin.shape[0] // N_CORES
    cos, sin = make_rope_tables()
    eye = np.eye(128, dtype=np.float32)
    shared = {
        "W_DKV": inputs["L0_attn_W_DKV"], "W_DQ": inputs["L0_attn_W_DQ"],
        "W_UK": inputs["L0_attn_W_UK"], "W_UV": inputs["L0_attn_W_UV"],
        "W_UQ": inputs["L0_attn_W_UQ"], "W_KR": inputs["L0_attn_W_KR"],
        "W_QR": inputs["L0_attn_W_QR"], "W_O": inputs["L0_attn_W_O"],
        "Wr": inputs["L0_ffn_Wr"], "rbias": inputs["L0_ffn_bias"],
        "Wg": inputs["L0_ffn_Wg"], "Wu": inputs["L0_ffn_Wu"],
        "Wd": inputs["L0_ffn_Wd"], "sWg": inputs["L0_ffn_sWg"],
        "sWu": inputs["L0_ffn_sWu"], "sWd": inputs["L0_ffn_sWd"],
        "eyef": eye, "ropec": cos, "ropes": sin,
    }
    shared = {k: np.ascontiguousarray(np.asarray(v, dtype=np.float32))
              for k, v in shared.items()}
    nc = build_program(nb)
    in_maps = []
    for cid in range(N_CORES):
        m = dict(shared)
        m["x"] = np.ascontiguousarray(xin[cid * nb:(cid + 1) * nb])
        in_maps.append(m)
    res = run_bass_kernel_spmd(nc, in_maps, core_ids=list(range(N_CORES)))
    return np.concatenate([np.asarray(r["out"]) for r in res.results], axis=0)
